# version 23
# baseline (speedup 1.0000x reference)
import contextlib
import numpy as np

import concourse.bass as bass
import concourse.mybir as mybir
from concourse.bass_utils import run_bass_kernel_spmd

F32 = mybir.dt.float32
BF16 = mybir.dt.bfloat16
ALU = mybir.AluOpType
AF = mybir.ActivationFunctionType
AX = mybir.AxisListType.X

INV_SQRT_HD = 0.08838834764831845
C_SCALE = 1.02 * INV_SQRT_HD / 2.0
LN_EPS = 1e-5
Q_EPS = 1e-8


def make_cfg(D=2048, S=2048, H=16, FH=5632, NB=2):
    c = dict(D=D, S=S, H=H, FH=FH, NB=NB)
    c["NT"] = D // 128
    c["NHT"] = FH // 128
    c["NKT"] = S // 128
    c["NQC"] = S // 512
    c["VG"] = 2
    c["NG"] = H // 2
    c["SQ"] = 512
    c["NSQ"] = S // 512
    return c


ENGSEM = {"sync": "Q", "gpsimd": "P", "vector": "V", "scalar": "S", "tensor": "T"}


DMA_POOL = 8


class Prog:

    def __init__(self):
        self.streams = {e: [] for e in ("sync", "gpsimd", "vector", "scalar", "tensor")}
        self.cnt = {}
        self.semnames = []
        for k in "VSTP":
            self.cnt[k] = 0
            self.semnames.append(k)
        for q in "QGC":
            for i in range(DMA_POOL):
                self.cnt[q + str(i)] = 0
                self.semnames.append(q + str(i))
        self.qrot = {q: 0 for q in "QGC"}
        self.waited = {e: {} for e in self.streams}
        self.bufs = {}
        self.pending = {}

    def _buf(self, name):
        return self.bufs.setdefault(name, {"w": None, "r": {}})

    def wait(self, eng, tok):
        sem, val = tok
        if val <= 0:
            return
        w = self.waited[eng]
        if w.get(sem, 0) >= val:
            return
        w[sem] = val
        self.streams[eng].append(("w", sem, val))

    def step(self, eng, fns, reads=(), writes=(), sem=None, inc=1):
        if callable(fns):
            fns = [fns]
        for b in reads:
            st = self._buf(b)
            if st["w"]:
                self.wait(eng, st["w"])
        for b in writes:
            for pref, toks in self.pending.items():
                if b.startswith(pref):
                    for tok in toks:
                        self.wait(eng, tok)
            st = self._buf(b)
            if st["w"]:
                self.wait(eng, st["w"])
            for s, v in st["r"].items():
                self.wait(eng, (s, v))
        if sem is None:
            sem = ENGSEM[eng]
        if sem in ("Q", "G", "C"):
            i = self.qrot[sem]
            self.qrot[sem] = (i + 1) % DMA_POOL
            sem = sem + str(i)
            self.wait(eng, (sem, self.cnt[sem]))
        self.cnt[sem] += inc
        tok = (sem, self.cnt[sem])
        for f in fns[:-1]:
            self.streams[eng].append(("i", f, None, 0))
        self.streams[eng].append(("i", fns[-1], sem, inc))
        for b in reads:
            st = self._buf(b)
            st["r"][tok[0]] = max(st["r"].get(tok[0], 0), tok[1])
        for b in writes:
            self.bufs[b] = {"w": tok, "r": {}}
        return tok

    def claim(self, prefix):
        toks = {}
        for name in [n for n in self.bufs if n.startswith(prefix)]:
            st = self.bufs.pop(name)
            if st["w"]:
                s, v = st["w"]
                toks[s] = max(toks.get(s, 0), v)
            for s, v in st["r"].items():
                toks[s] = max(toks.get(s, 0), v)
        self.pending[prefix] = list(toks.items())


def build_program(cfg, debug=False):
    D, S, H, FH, NB = cfg["D"], cfg["S"], cfg["H"], cfg["FH"], cfg["NB"]
    NT, NHT, NKT, NQC = cfg["NT"], cfg["NHT"], cfg["NKT"], cfg["NQC"]
    VG, NG, SQ, NSQ = cfg["VG"], cfg["NG"], cfg["SQ"], cfg["NSQ"]
    PCW = min(2048, S)
    W3C = (FH + PCW - 1) // PCW
    assert S == D, "R3 role layout assumes S == D"

    nc = bass.Bass("TRN2", target_bir_lowering=False)

    xt_d = nc.dram_tensor("xt", [NB, D, S], F32, kind="ExternalInput").ap()
    wq_d = nc.dram_tensor("wq_p", [H, 128, D], F32, kind="ExternalInput").ap()
    wk_d = nc.dram_tensor("wk_p", [H, 128, D], F32, kind="ExternalInput").ap()
    wv_d = nc.dram_tensor("wv_p", [H, 128, D], F32, kind="ExternalInput").ap()
    wo_d = nc.dram_tensor("wo_p", [NT, 128, D], F32, kind="ExternalInput").ap()
    w1_d = nc.dram_tensor("w1_p", [NHT, 128, D], F32, kind="ExternalInput").ap()
    w2_d = nc.dram_tensor("w2_p", [NHT, 128, D], F32, kind="ExternalInput").ap()
    w3_d = nc.dram_tensor("w3_p", [NT, 128, FH], F32, kind="ExternalInput").ap()
    bq_d = nc.dram_tensor("bq_p", [128, H], F32, kind="ExternalInput").ap()
    bk_d = nc.dram_tensor("bk_p", [128, H], F32, kind="ExternalInput").ap()
    bv_d = nc.dram_tensor("bv_p", [128, H], F32, kind="ExternalInput").ap()
    bo_d = nc.dram_tensor("bo_p", [128, NT], F32, kind="ExternalInput").ap()
    trig_d = nc.dram_tensor("trig", [128, S], F32, kind="ExternalInput").ap()
    out_d = nc.dram_tensor("out", [NB, D, S], F32, kind="ExternalOutput").ap()
    w1q_d = nc.dram_tensor("w1q", [NHT, 128, D], BF16, kind="Internal").ap()
    w2q_d = nc.dram_tensor("w2q", [NHT, 128, D], BF16, kind="Internal").ap()
    w3q_d = nc.dram_tensor("w3q", [NT, 128, FH], BF16, kind="Internal").ap()
    x1sp_d = nc.dram_tensor("x1sp", [NT, 128, S], F32,
                            kind="ExternalOutput" if debug else "Internal").ap()
    osp_d = nc.dram_tensor("osp", [NT, 128, S], BF16, kind="Internal").ap()
    ysp_d = nc.dram_tensor("ysp", [NT, 128, S], BF16, kind="Internal").ap()
    if debug:
        ospdbg_d = nc.dram_tensor("ospdbg", [NT, 128, S], F32, kind="ExternalOutput").ap()
        yspdbg_d = nc.dram_tensor("yspdbg", [NT, 128, S], F32, kind="ExternalOutput").ap()
        w1qdbg_d = nc.dram_tensor("w1qdbg", [NHT, 128, D], F32, kind="ExternalOutput").ap()
        w2qdbg_d = nc.dram_tensor("w2qdbg", [NHT, 128, D], F32, kind="ExternalOutput").ap()
        w3qdbg_d = nc.dram_tensor("w3qdbg", [NT, 128, FH], F32, kind="ExternalOutput").ap()
        dbg_q = {"w1": w1qdbg_d, "w2": w2qdbg_d, "w3": w3qdbg_d}

    cur = [((nc.sbuf_base + 31) // 32 * 32)]
    limit = nc.SBUF_PARTITION_SIZE_BYTES - 32

    def carve(name, shape, dtype):
        h = nc.alloc_sbuf_tensor_at(name, shape, dtype, offset=cur[0])
        per = 1
        for s in shape[1:]:
            per *= s
        per *= mybir.dt.size(dtype)
        cur[0] += (per + 31) // 32 * 32
        assert cur[0] <= limit, f"SBUF overflow at {name}: {cur[0]} > {limit}"
        return h

    R1 = carve("R1", [128, NT * S], BF16)
    R2 = carve("R2", [128, max(NT * S, NHT * SQ + 2 * FH)], BF16)
    R3 = carve("R3", [128, 4 * D], BF16)
    R4 = carve("R4", [128, max(2 * S, 5 * SQ)], BF16)
    R5 = carve("R5", [128, S], BF16)
    S_R6 = max(2048, S)
    R6 = carve("R6", [128, S_R6], BF16)
    R8a = carve("R8a", [128, S], F32)
    R8b = carve("R8b", [128, S], F32)
    R9a = carve("R9a", [128, S], F32)
    R9b = carve("R9b", [128, 512], F32)
    R9c = carve("R9c", [128, S], BF16)
    R10 = carve("R10", [128, max(2048, S)], F32)
    trig_sb = carve("trig_sb", [128, S], BF16)
    ones_bf = carve("ones_bf", [128, 128], BF16)
    onesrow = carve("onesrow", [1, 128], F32)
    onescol = carve("onescol", [128, 1], F32)
    epscol = carve("epscol", [128, 1], F32)
    bq_sb = carve("bq_sb", [128, H], F32)
    bk_sb = carve("bk_sb", [128, H], F32)
    bv_sb = carve("bv_sb", [128, H], F32)
    bo_sb = carve("bo_sb", [128, NT], F32)
    gpart = carve("gpart", [128, 64], F32)
    smallc = carve("smallc", [128, 8], F32)
    smallr = carve("smallr", [1, 16], F32)

    P = Prog()
    P_SEMNAMES = P.semnames
    ctxmgr = contextlib.ExitStack()
    PA = ctxmgr.enter_context(nc.psum_tensor("PA", [128, 2048], F32))
    PB = ctxmgr.enter_context(nc.psum_tensor("PB", [128, 2048], F32))
    sems = {k: ctxmgr.enter_context(nc.semaphore("sem" + k)) for k in P_SEMNAMES}

    def PAs(i):
        return PA[:, i * 512:(i + 1) * 512]

    def PBs(i):
        return PB[:, i * 512:(i + 1) * 512]

    P.step("vector", lambda e: e.memset(ones_bf[:], 1.0), writes=("ones_bf",))
    P.step("vector", lambda e: e.memset(onesrow[:], 1.0), writes=("onesrow",))
    P.step("vector", lambda e: e.memset(onescol[:], 1.0), writes=("onescol",))
    P.step("vector", lambda e: e.memset(epscol[:], LN_EPS), writes=("epscol",))
    P.step("gpsimd", lambda e: e.dma_start(trig_sb[:], trig_d[:]), writes=("trig",), sem="G", inc=16)
    P.step("sync", lambda e: e.dma_start(bq_sb[:], bq_d[:]), writes=("bq",), sem="Q", inc=16)
    P.step("sync", lambda e: e.dma_start(bk_sb[:], bk_d[:]), writes=("bk",), sem="Q", inc=16)
    P.step("sync", lambda e: e.dma_start(bv_sb[:], bv_d[:]), writes=("bv",), sem="Q", inc=16)
    P.step("sync", lambda e: e.dma_start(bo_sb[:], bo_d[:]), writes=("bo",), sem="Q", inc=16)

    t1l = [(w1_d[i], D, w1q_d[i], f"w1q{i}") for i in range(NHT)]
    t2l = [(w2_d[i], D, w2q_d[i], f"w2q{i}") for i in range(NHT)]
    t3l = []
    for oc in range(NT):
        for j in range(W3C):
            c0, c1 = j * PCW, min((j + 1) * PCW, FH)
            t3l.append((w3_d[oc][:, c0:c1], c1 - c0, w3q_d[oc][:, c0:c1], f"w3q{oc}_{j}"))
    prep_mats = [("w1", t1l, NHT * 128 * D), ("w2", t2l, NHT * 128 * D),
                 ("w3", t3l, NT * 128 * FH)]
    scratch_reads = {}

    def quarters(ncols):
        qs = []
        c = 0
        while c < ncols:
            qs.append((c, min(ncols, c + 512)))
            c += 512
        return qs

    slot_ctr = [0]

    def alloc_slot(w, cell):
        slot = 1 + slot_ctr[0] % 3
        slot_ctr[0] += 1
        cell[0] = slot
        return R10[:, slot * 512: slot * 512 + w], f"R10q{slot}"

    def slot_of(w, cell):
        slot = cell[0]
        return R10[:, slot * 512: slot * 512 + w], f"R10q{slot}"

    def gamma_quarter_tasks(tile_idx, gq0, tile):
        src, ncols, _, _ = tile
        qs = quarters(ncols)
        tasks = []
        nq = len(qs)
        for qi, (c0, c1) in enumerate(qs):
            cell = [None]

            def p0(src=src, c0=c0, c1=c1, w=c1 - c0, cell=cell):
                stg, stn = alloc_slot(w, cell)
                P.step("sync", lambda e: e.dma_start(stg, src[:, c0:c1]),
                       writes=(stn,), sem="Q", inc=16)

            def p1(qi=qi, w=c1 - c0, cell=cell, tile_idx=tile_idx,
                   last=(qi == len(qs) - 1), nq=nq):
                stg, stn = slot_of(w, cell)
                P.step("vector",
                       lambda e: e.tensor_reduce(gpart[:, 56 + qi % 4: 57 + qi % 4],
                                                 stg, AX, ALU.add,
                                                 apply_absolute_value=True),
                       reads=(stn,), writes=("gpartq",))
                if last:
                    P.step("vector",
                           lambda e: e.tensor_reduce(
                               gpart[:, tile_idx:tile_idx + 1],
                               gpart[:, 56:56 + nq], AX, ALU.add),
                           reads=("gpartq",), writes=("gpart",))
            tasks.append((p0, p1))
        return tasks

    def gamma_fin_tasks(mi, ntiles, nelem):
        c4, c5, c6 = 4 + 3 * mi, 5 + 3 * mi, 6 + 3 * mi
        gc = 60 + mi

        def pA():
            P.step("vector",
                   lambda e: e.tensor_reduce(gpart[:, gc:gc + 1], gpart[:, 0:ntiles], AX, ALU.add),
                   reads=("gpart",), writes=(f"gfin{mi}",))

        def pB():
            P.step("tensor",
                   lambda e: e.matmul(PBs(0)[0:1, 0:1], onescol[:], gpart[:, gc:gc + 1],
                                      start=True, stop=True),
                   reads=(f"gfin{mi}", "onescol"), writes=("pb0",))
            P.step("scalar", lambda e: e.copy(smallr[0:1, c4:c4 + 1], PBs(0)[0:1, 0:1]),
                   reads=("pb0",), writes=(f"gsum{mi}",))
            P.step("vector",
                   lambda e: e.tensor_scalar(smallr[0:1, c5:c5 + 1], smallr[0:1, c4:c4 + 1],
                                             1.0 / nelem, Q_EPS, ALU.mult, ALU.add),
                   reads=(f"gsum{mi}",), writes=(f"gden{mi}",))
            P.step("vector", lambda e: e.reciprocal(smallr[0:1, c6:c6 + 1], smallr[0:1, c5:c5 + 1]),
                   reads=(f"gden{mi}",), writes=(f"grec{mi}",))
            P.step("tensor",
                   lambda e: e.matmul(PBs(0)[:, 0:1], onesrow[:], smallr[0:1, c6:c6 + 1],
                                      start=True, stop=True),
                   reads=(f"grec{mi}", "onesrow"), writes=("pb0",))
            P.step("scalar", lambda e: e.copy(smallc[:, 1 + mi:2 + mi], PBs(0)[:, 0:1]),
                   reads=("pb0",), writes=(f"rgcol{mi}",))
        return ((lambda: None, pA), (lambda: None, pB))

    def quant_quarter_tasks(mi, qidx0, tile, dbg_dst=None):
        src, ncols, dst, bufname = tile
        qs = quarters(ncols)
        names = []
        tasks = []
        for qi, (c0, c1) in enumerate(qs):
            w = c1 - c0
            qn = f"{bufname}q{qi}"
            names.append(qn)
            cell = [None]

            def p0(src=src, c0=c0, c1=c1, w=w, cell=cell):
                stg, stn = alloc_slot(w, cell)
                P.step("sync", lambda e: e.dma_start(stg, src[:, c0:c1]),
                       writes=(stn,), sem="Q", inc=16)

            def p1(c0=c0, c1=c1, w=w, cell=cell, qn=qn, dbg_dst=dbg_dst):
                stg, stn = slot_of(w, cell)
                P.step("vector",
                       lambda e: e.tensor_scalar(
                           R9b[:, 0:w], stg, smallc[:, 1 + mi:2 + mi], -0.5,
                           ALU.mult, ALU.is_le),
                       reads=(stn, f"rgcol{mi}"), writes=("R9b",))
                P.step("vector",
                       lambda e: e.tensor_scalar(
                           stg, stg, smallc[:, 1 + mi:2 + mi], 0.5,
                           ALU.mult, ALU.is_ge),
                       reads=(stn, f"rgcol{mi}"), writes=(stn,))
                P.step("vector",
                       lambda e: e.tensor_sub(stg, stg, R9b[:, 0:w]),
                       reads=(stn, "R9b"), writes=(stn,))
                P.step("gpsimd", lambda e: e.dma_start(dst[:, c0:c1], stg),
                       reads=(stn,), writes=(qn,), sem="G", inc=16)
                if dbg_dst is not None:
                    P.step("sync", lambda e: e.dma_start(dbg_dst[:, c0:c1], stg),
                           reads=(stn,), writes=(qn + "dbg",), sem="Q", inc=16)
            tasks.append((p0, p1))
        scratch_reads.setdefault(bufname, tuple(names))
        return tasks

    class SplitTasks:
        def __init__(self, tasks, look=3):
            self.tasks = tasks
            self.look = look
            self.h0 = 0
            self.h1 = 0

        def pop(self, n):
            for _ in range(n):
                while self.h0 < len(self.tasks) and self.h0 < self.h1 + self.look:
                    self.tasks[self.h0][0]()
                    self.h0 += 1
                if self.h1 < len(self.tasks):
                    self.tasks[self.h1][1]()
                    self.h1 += 1

        def flush(self):
            self.pop(len(self.tasks))

        def sync_point(self):
            while self.h1 < self.h0:
                self.tasks[self.h1][1]()
                self.h1 += 1

    gammas = {}
    fins = {}
    quants = {}
    qct = 0
    gqc = 0
    for mi, (nm, tiles, nelem) in enumerate(prep_mats):
        gtl = []
        for i, tile in enumerate(tiles):
            gtl.extend(gamma_quarter_tasks(i, gqc, tile))
            gqc += len(quarters(tile[1]))
        gammas[nm] = gtl
        fins[nm] = gamma_fin_tasks(mi, len(tiles), nelem)
        qs = []
        for ti, tile in enumerate(tiles):
            dd = None
            if debug:
                src, ncols, dst, bufname = tile
                if nm in ("w1", "w2"):
                    dd = dbg_q[nm][ti]
                else:
                    oc, j = ti // W3C, ti % W3C
                    dd = dbg_q["w3"][oc][:, j * PCW:j * PCW + ncols]
            qs.extend(quant_quarter_tasks(mi, qct, tile, dd))
            qct += len(quarters(tile[1]))
        quants[nm] = qs
    gl = []
    pendB = []

    def push(t):
        gl.append(t)
        for p in pendB:
            p[0] -= 1
        while pendB and pendB[0][0] <= 0:
            gl.append(pendB.pop(0)[1])

    def pushfin(nm):
        gl.append(fins[nm][0])
        pendB.append([12, fins[nm][1]])

    for t in gammas["w1"]:
        push(t)
    pushfin("w1")
    rb = (list(gammas["w2"]) + [("F", "w2")]
          + list(gammas["w3"]) + [("F", "w3")])
    while rb:
        t = rb.pop(0)
        if isinstance(t, tuple) and t[0] == "F":
            pushfin(t[1])
        else:
            push(t)
    while pendB:
        gl.append(pendB.pop(0)[1])
    att_tasks = SplitTasks(gl, look=2)
    q1, q2 = quants.get("w1", []), quants.get("w2", [])
    ml = []
    i1 = i2 = 0
    while i1 < len(q1) or i2 < len(q2):
        ml.extend(q1[i1:i1 + 4])
        i1 += 4
        ml.extend(q2[i2:i2 + 4])
        i2 += 4
    mid_tasks = SplitTasks(ml, look=2)
    ffn_tasks = SplitTasks(quants.get("w3", []), look=2)

    def ln_stats_pipeline(n_tiles, srcgen, pops=None):
        exp_all = tuple(f"exp{i}" for i in range(4))
        pend = []
        pairs = (
            (R5[:, 0:S], ("R5full",), R6[:, 0:S], exp_all),
            (R4[:, 0:S], ("R4a",), R4[:, S:2 * S], ("R4b",)),
            (R2[:, 0:S], ("R2st0",), R2[:, S:2 * S], ("R2st1",)),
            (R2[:, 2 * S:3 * S], ("R2st2",), R2[:, 3 * S:4 * S], ("R2st3",)),
        )
        for t in range(n_tiles):
            src_ap, src_reads, is_f32 = srcgen(t)
            cb, cbn, sqb, sqn = pairs[t % 4]
            if is_f32:
                P.step("vector", lambda e, cb=cb, src_ap=src_ap: e.tensor_copy(cb, src_ap),
                       reads=tuple(src_reads), writes=cbn)
                sum_ap, sum_reads = cb, cbn
            else:
                sum_ap, sum_reads = src_ap, tuple(src_reads)
            P.step("scalar", lambda e, sqb=sqb, src_ap=src_ap: e.activation(
                sqb, src_ap, AF.Square, bias=0.0, scale=1.0),
                reads=tuple(src_reads), writes=sqn)
            while len(pend) > 1:
                pend.pop(0)()
            if pops is not None:
                pops()

            def mk(t=t, sum_ap=sum_ap, sum_reads=sum_reads, sqb=sqb, sqn=sqn):
                def go():
                    for qc in range(NQC):
                        P.step("tensor",
                               lambda e, qc=qc: e.matmul(
                                   PAs(qc), ones_bf[:], sum_ap[:, qc * 512:(qc + 1) * 512],
                                   start=(t == 0), stop=(t == n_tiles - 1),
                                   skip_group_check=True),
                               reads=("ones_bf",) + tuple(sum_reads), writes=(f"pa{qc}",))
                        P.step("tensor",
                               lambda e, qc=qc: e.matmul(
                                   PBs(qc), ones_bf[:], sqb[:, qc * 512:(qc + 1) * 512],
                                   start=(t == 0), stop=(t == n_tiles - 1),
                                   skip_group_check=True),
                               reads=("ones_bf",) + tuple(sqn), writes=(f"pb{qc}",))
                return go
            pend.append(mk())
        while pend:
            pend.pop(0)()

    def ln_rows():
        for qc in range(NQC):
            c0, c1 = qc * 512, (qc + 1) * 512
            P.step("vector",
                   lambda e, c0=c0, c1=c1, qc=qc: e.tensor_scalar(
                       R9a[:, c0:c1], PBs(qc), 1.0 / D, None, ALU.mult),
                   reads=(f"pb{qc}",), writes=("bcr",))
            P.step("vector",
                   lambda e, c0=c0, c1=c1, qc=qc: e.tensor_scalar(
                       R8b[:, c0:c1], PAs(qc), 1.0 / D, None, ALU.mult),
                   reads=(f"pa{qc}",), writes=("R8b",))
            P.step("vector",
                   lambda e, c0=c0, c1=c1: e.tensor_mul(
                       R9b[:, 0:512], R8b[:, c0:c1], R8b[:, c0:c1]),
                   reads=("R8b",), writes=("R9b",))
            P.step("vector",
                   lambda e, c0=c0, c1=c1: e.tensor_sub(
                       R9a[:, c0:c1], R9a[:, c0:c1], R9b[:, 0:512]),
                   reads=("bcr", "R9b"), writes=("bcr",))
            P.step("scalar",
                   lambda e, c0=c0, c1=c1: e.activation(
                       R9b[:, 0:512], R9a[:, c0:c1], AF.Ln, bias=epscol[:], scale=1.0),
                   reads=("bcr", "epscol"), writes=("R9b",))
            P.step("scalar",
                   lambda e, c0=c0, c1=c1: e.activation(
                       R9a[:, c0:c1], R9b[:, 0:512], AF.Exp, bias=0.0, scale=-0.5),
                   reads=("R9b",), writes=("bcr",))
            P.step("vector",
                   lambda e, c0=c0, c1=c1: e.tensor_mul(
                       R9c[:, c0:c1], R8b[:, c0:c1], R9a[:, c0:c1]),
                   reads=("R8b", "bcr"), writes=("bcmr",))

    xn_names = tuple(f"R1_t{t}" for t in range(NT))
    ctx_names = tuple(f"R2_h{h}" for h in range(H))
    hid_names = tuple(f"R2s{hc}" for hc in range(NHT))
    vtok_names = tuple(f"vtok{tt}" for tt in range(NKT))

    for b in range(NB):
        P.claim("R4")
        P.claim("R5")
        P.claim("R2")

        def l1_src(t):
            x, xn = (R8a, "R8a") if t % 2 == 0 else (R8b, "R8b")
            P.step("sync", lambda e, t=t, x=x, b=b: e.dma_start(
                x[:], xt_d[b][t * 128:(t + 1) * 128, :]),
                writes=(xn,), sem="Q", inc=16)
            return x[:], (xn,), True
        ln_stats_pipeline(NT, l1_src)
        ln_rows()
        for t in range(NT):
            x, xn = (R8a, "R8a") if t % 2 == 0 else (R8b, "R8b")
            P.step("sync", lambda e, t=t, x=x, b=b: e.dma_start(
                x[:], xt_d[b][t * 128:(t + 1) * 128, :]),
                writes=(xn,), sem="Q", inc=16)
            sl = R1[:, t * S:(t + 1) * S]
            P.step("vector", lambda e, sl=sl, x=x: e.tensor_mul(sl, x[:], R9a[:]),
                   reads=(xn, "bcr"), writes=(f"R1_t{t}",))
            P.step("vector", lambda e, sl=sl: e.tensor_sub(sl, sl, R9c[:]),
                   reads=(f"R1_t{t}", "bcmr"), writes=(f"R1_t{t}",))

        P.claim("R3")
        P.claim("R4")
        P.claim("R5")
        P.claim("R2")
        P.claim("pa1")

        def panm(pnm, qc):
            if pnm == "pa" and qc == 1:
                return ("pa1r0", "pa1r64")
            return (f"{pnm}{qc}",)
        def emit_wload(h):
            P.step("gpsimd", lambda e, h=h: e.dma_start(R3[:, 0:D], wq_d[h]),
                   writes=("R3_wq",), sem="G", inc=16)
            P.step("gpsimd", lambda e, h=h: e.dma_start(R3[:, D:2 * D], wk_d[h]),
                   writes=("R3_wk",), sem="G", inc=16)
            P.step("gpsimd", lambda e, h=h: e.dma_start(R3[:, 2 * D:3 * D], wv_d[h]),
                   writes=("R3_wv",), sem="G", inc=16)

        emit_wload(0)
        prev_chain = None
        chain_defer_qc = 1 if ((NQC - 1) % 2 == 1 and NQC >= 4) else None
        for h in range(H):
            if chain_defer_qc is None and prev_chain is not None:
                prev_chain()
                prev_chain = None
                if b == 0:
                    att_tasks.pop(9)
            qk_group_i = 0
            for wbufn, wofs, PS, PSfull, pnm, qkT, qkn, bsb, bn in (
                    ("R3_wq", 0, PAs, PA, "pa", R4[:, 0:S], "R4_q", bq_sb, "bq"),
                    ("R3_wk", D, PBs, PB, "pb", R4[:, S:2 * S], "R4_k", bk_sb, "bk")):
                for qc in range(NQC):
                    fns = []
                    for d in range(NT):
                        fns.append(lambda e, d=d, qc=qc, PS=PS, wofs=wofs: e.matmul(
                            PS(qc),
                            R3[:, wofs + d * 128: wofs + (d + 1) * 128],
                            R1[:, d * S + qc * 512: d * S + (qc + 1) * 512],
                            start=(d == 0), stop=(d == NT - 1)))
                    P.step("tensor", fns, reads=(wbufn,) + xn_names,
                           writes=panm(pnm, qc))
                    if qk_group_i == chain_defer_qc and prev_chain is not None:
                        prev_chain()
                        prev_chain = None
                        if b == 0:
                            att_tasks.pop(9)
                    qk_group_i += 1
                P.step("scalar",
                       lambda e, qkT=qkT, PSfull=PSfull, bsb=bsb, h=h: e.activation(
                           qkT, PSfull[:, 0:NQC * 512], AF.Identity,
                           bias=bsb[:, h:h + 1], scale=1.0),
                       reads=sum((panm(pnm, qc) for qc in range(NQC)), ()) + (bn,),
                       writes=(qkn,))
                rsc = R2[0:64, h * S:(h + 1) * S]
                P.step("vector",
                       lambda e, qkT=qkT, rsc=rsc: e.tensor_mul(
                           rsc, qkT[64:128, :], trig_sb[64:128, :]),
                       reads=(qkn, "trig"), writes=(f"R2_h{h}",))
                P.step("vector",
                       lambda e, qkT=qkT: e.tensor_mul(
                           qkT[0:64, :], qkT[0:64, :], trig_sb[0:64, :]),
                       reads=(qkn, "trig"), writes=(qkn,))
                P.step("vector",
                       lambda e, qkT=qkT, rsc=rsc: e.tensor_sub(
                           qkT[0:64, :], qkT[0:64, :], rsc),
                       reads=(qkn, f"R2_h{h}"), writes=(qkn,))

            for qkT, qkn, PS, PSfull, pnm, ridx in (
                    (R4[:, 0:S], "R4_q", PAs, PA, "pa", 0),
                    (R4[:, S:2 * S], "R4_k", PBs, PB, "pb", 1)):
                for qc in range(NQC):
                    slot = qc % 2
                    P.step("scalar",
                           lambda e, qc=qc, slot=slot, qkT=qkT: e.activation(
                               R6[:, slot * 512:(slot + 1) * 512],
                               qkT[:, qc * 512:(qc + 1) * 512], AF.Square,
                               bias=0.0, scale=1.0),
                           reads=(qkn,), writes=(f"exp{slot}",))
                    P.step("tensor",
                           lambda e, qc=qc, slot=slot, PS=PS: e.matmul(
                               PS(qc)[0:1, :], ones_bf[:, 0:1],
                               R6[:, slot * 512:(slot + 1) * 512],
                               start=True, stop=True),
                           reads=(f"exp{slot}", "ones_bf"),
                           writes=(("pa1r0",) if (pnm == "pa" and qc == 1)
                                   else (f"{pnm}{qc}",)))
                P.step("vector",
                       lambda e, PSfull=PSfull, ridx=ridx: e.reduce_max(
                           smallr[0:1, ridx:ridx + 1], PSfull[0:1, 0:NQC * 512], AX),
                       reads=tuple(("pa1r0" if (pnm == "pa" and qc == 1)
                                    else f"{pnm}{qc}") for qc in range(NQC)),
                       writes=(f"mx{ridx}",))
            P.step("vector",
                   lambda e: e.tensor_add(smallr[0:1, 2:3], smallr[0:1, 0:1],
                                          smallr[0:1, 1:2]),
                   reads=("mx0", "mx1"), writes=("c0",))
            P.step("vector",
                   lambda e: e.tensor_scalar(smallr[0:1, 3:4], smallr[0:1, 2:3],
                                             -C_SCALE, -1.0, ALU.mult, ALU.add),
                   reads=("c0",), writes=("c1",))

            for tt in range(NKT):
                ps = PAs(3) if tt % 2 == 0 else PBs(3)
                pn = "pa3" if tt % 2 == 0 else "pb3"
                fns = []
                for d in range(NT):
                    fns.append(lambda e, d=d, tt=tt, ps=ps: e.matmul(
                        ps[:, 0:128],
                        R1[:, d * S + tt * 128: d * S + tt * 128 + 128],
                        R3[:, 2 * D + d * 128: 2 * D + (d + 1) * 128],
                        start=(d == 0), stop=(d == NT - 1)))
                P.step("tensor", fns, reads=("R3_wv",) + xn_names, writes=(pn,))
                P.step("scalar",
                       lambda e, tt=tt, ps=ps: e.activation(
                           R5[:, tt * 128:(tt + 1) * 128], ps[:, 0:128],
                           AF.Copy, bias=0.0, scale=1.0),
                       reads=(pn,), writes=(f"R5v{tt}",))
            if h + 1 < H:
                emit_wload(h + 1)

            P.step("tensor",
                   lambda e: e.matmul(PAs(0)[:, 0:1], onesrow[:], smallr[0:1, 3:4],
                                      start=True, stop=True),
                   reads=("c1", "onesrow"), writes=("pa0",))
            P.step("scalar", lambda e: e.copy(smallc[:, 0:1], PAs(0)[:, 0:1]),
                   reads=("pa0",), writes=("cbias",))

            def make_chain(qc, cpn, cps, h=h):
                sl = R2[:, h * S + qc * 512: h * S + (qc + 1) * 512]

                def chain():
                    P.step("tensor",
                           lambda e: e.matmul(PAs(2), onesrow[:], R10[0:1, 0:512],
                                              start=True, stop=True),
                           reads=("R10", "onesrow"), writes=("pa2",))
                    P.step("vector",
                           lambda e: e.tensor_scalar(R10[:, 0:512], PAs(2), 1.0,
                                                     None, ALU.mult),
                           reads=("pa2",), writes=("R10",))
                    P.step("vector", lambda e, cps=cps: e.tensor_mul(sl, cps, R10[:, 0:512]),
                           reads=(cpn, "R10"), writes=(f"R2_h{h}",))
                    P.step("vector",
                           lambda e: e.tensor_scalar(
                               sl, sl, bv_sb[:, h:h + 1], None, ALU.add),
                           reads=(f"R2_h{h}", "bv"), writes=(f"R2_h{h}",))
                return chain

            LOOKAHEAD = 3
            for qc in range(NQC):
                cpn = "pa0" if qc % 2 == 0 else "pa3"
                cps = PAs(0) if qc % 2 == 0 else PAs(3)
                dr = 0 if qc % 2 == 0 else 64
                dps = PAs(1)[dr:dr + 1, :]
                dpn = f"pa1r{dr}"
                for step in range(NKT + LOOKAHEAD):
                    if step < NKT:
                        kt, pb = step, step % 4
                        P.step("tensor",
                               lambda e, kt=kt, qc=qc, pb=pb: e.matmul(
                                   PBs(pb), R4[:, S + kt * 128: S + (kt + 1) * 128],
                                   R4[:, qc * 512:(qc + 1) * 512],
                                   start=True, stop=True),
                               reads=("R4_q", "R4_k"), writes=(f"pb{pb}",))
                        if kt % 2 == 1:
                            b0 = (kt - 1) % 4
                            P.step("scalar",
                                   lambda e, b0=b0: e.activation(
                                       R6[:, b0 * 512:b0 * 512 + 1024],
                                       PB[:, b0 * 512:b0 * 512 + 1024], AF.Exp,
                                       bias=smallc[:, 0:1], scale=INV_SQRT_HD),
                                   reads=(f"pb{b0}", f"pb{b0 + 1}", "cbias"),
                                   writes=(f"exp{b0}", f"exp{b0 + 1}"))
                    if step >= LOOKAHEAD:
                        kt = step - LOOKAHEAD
                        slot = kt % 4
                        P.step("tensor",
                               lambda e, kt=kt, slot=slot, dps=dps: e.matmul(
                                   dps, ones_bf[:, 0:1],
                                   R6[:, slot * 512:(slot + 1) * 512],
                                   start=(kt == 0), stop=(kt == NKT - 1),
                                   skip_group_check=True),
                               reads=(f"exp{slot}", "ones_bf"), writes=(dpn,))
                        P.step("tensor",
                               lambda e, kt=kt, slot=slot, cps=cps: e.matmul(
                                   cps, R5[:, kt * 128:(kt + 1) * 128],
                                   R6[:, slot * 512:(slot + 1) * 512],
                                   start=(kt == 0), stop=(kt == NKT - 1),
                                   skip_group_check=True),
                               reads=(f"exp{slot}", f"R5v{kt}"), writes=(cpn,))
                        if kt == 1 and prev_chain is not None:
                            prev_chain()
                            prev_chain = None
                            if b == 0:
                                att_tasks.pop(9)
                P.step("scalar",
                       lambda e, dps=dps: e.activation(
                           R8b[0:1, 0:512], dps, AF.Ln,
                           bias=0.0, scale=1.0),
                       reads=(dpn,), writes=("R8b",))
                P.step("scalar",
                       lambda e: e.activation(
                           R10[0:1, 0:512], R8b[0:1, 0:512], AF.Exp,
                           bias=0.0, scale=-1.0),
                       reads=("R8b",), writes=("R10",))
                prev_chain = make_chain(qc, cpn, cps)
        prev_chain()
        prev_chain = None
        P.claim("pa1")

        if b == 0:
            att_tasks.flush()

        P.claim("R3")
        P.claim("R10")
        def wo_load(oc):
            ws = oc % 2
            wofs = ws * D
            P.step("gpsimd", lambda e, oc=oc, wofs=wofs: e.dma_start(
                R3[:, wofs:wofs + D], wo_d[oc]),
                writes=(f"R3_wo{ws}",), sem="G", inc=16)

        wo_load(0)
        for oc in range(NT):
            ws = oc % 2
            wofs = ws * D
            if oc + 1 < NT:
                wo_load(oc + 1)
            PS, PSfull, pnm = (PAs, PA, "pa") if oc % 2 == 0 else (PBs, PB, "pb")
            for qc in range(NQC):
                fns = []
                for d in range(NT):
                    fns.append(lambda e, d=d, qc=qc, PS=PS, wofs=wofs: e.matmul(
                        PS(qc), R3[:, wofs + d * 128: wofs + (d + 1) * 128],
                        R2[:, d * S + qc * 512: d * S + (qc + 1) * 512],
                        start=(d == 0), stop=(d == NT - 1)))
                P.step("tensor", fns, reads=(f"R3_wo{ws}",) + ctx_names,
                       writes=(f"{pnm}{qc}",))
            oofs = 2 * D + ws * D
            P.step("scalar",
                   lambda e, oofs=oofs, PSfull=PSfull, oc=oc: e.activation(
                       R3[:, oofs:oofs + S], PSfull[:, 0:NQC * 512],
                       AF.Identity, bias=bo_sb[:, oc:oc + 1], scale=1.0),
                   reads=tuple(f"{pnm}{qc}" for qc in range(NQC)) + ("bo",),
                   writes=(f"R3_os{ws}",))
            P.step("sync", lambda e, oofs=oofs, oc=oc: e.dma_start(
                osp_d[oc], R3[:, oofs:oofs + S]),
                reads=(f"R3_os{ws}",), writes=(f"osp{oc}",), sem="Q", inc=16)
            if b == 0:
                mid_tasks.pop(14)
            if debug:
                P.step("gpsimd", lambda e, oofs=oofs, oc=oc: e.dma_start(
                    ospdbg_d[oc], R3[:, oofs:oofs + S]),
                    reads=(f"R3_os{ws}",), writes=(f"ospdbg{oc}",), sem="G", inc=16)

        if b == 0:
            mid_tasks.sync_point()

        P.claim("R4")
        P.claim("R5")
        P.claim("R2")

        def ln2_src(t):
            sl = R2[:, (4 + t % 4) * S:(5 + t % 4) * S]
            osn = f"R2os{t % 4}"
            P.step("sync", lambda e, sl=sl, t=t: e.dma_start(sl, osp_d[t]),
                reads=(f"osp{t}",), writes=(osn,), sem="Q", inc=16)
            return sl, (osn,), False
        ln_stats_pipeline(NT, ln2_src)
        ln_rows()

        if b == 0:
            mid_tasks.sync_point()
        P.claim("R10")

        def res1_src(t):
            oofs = 2 * D + (t % 2) * D
            osn = f"R3_os{t % 2}"
            P.step("sync", lambda e, t=t, oofs=oofs: e.dma_start(
                R3[:, oofs:oofs + S], osp_d[t]),
                reads=(f"osp{t}",), writes=(osn,), sem="Q", inc=16)
            P.step("sync", lambda e, t=t, b=b: e.dma_start(
                R8a[:], xt_d[b][t * 128:(t + 1) * 128, :]),
                writes=("R8a",), sem="Q", inc=16)
            fw, fwn = (R10[:, 0:S], "R10") if t % 2 == 0 else (R8b[:], "R8b")
            P.step("gpsimd",
                   lambda e, fw=fw, oofs=oofs: e.tensor_mul(
                       fw[:], R3[:, oofs:oofs + S], R9a[:]),
                   reads=(osn, "bcr"), writes=(fwn,))
            P.step("vector", lambda e, fw=fw: e.tensor_sub(fw[:], fw[:], R9c[:]),
                   reads=(fwn, "bcmr"), writes=(fwn,))
            P.step("vector", lambda e, fw=fw: e.tensor_add(fw[:], fw[:], R8a[:]),
                   reads=(fwn, "R8a"), writes=(fwn,))
            P.step("sync", lambda e, fw=fw, t=t: e.dma_start(x1sp_d[t], fw[:]),
                   reads=(fwn,), writes=(f"x1sp{t}",), sem="Q", inc=16)
            return fw[:], (fwn,), True
        ln_stats_pipeline(NT, res1_src)
        ln_rows()

        P.claim("R10")
        for t in range(NT):
            x, xn = (R8a, "R8a") if t % 2 == 0 else (R8b, "R8b")
            P.step("sync", lambda e, t=t, x=x: e.dma_start(x[:], x1sp_d[t]),
                   reads=(f"x1sp{t}",), writes=(xn,), sem="Q", inc=16)
            sl = R1[:, t * S:(t + 1) * S]
            P.step("gpsimd", lambda e, sl=sl, x=x: e.tensor_mul(sl, x[:], R9a[:]),
                   reads=(xn, "bcr"), writes=(f"R1_t{t}",))
            P.step("vector", lambda e, sl=sl: e.tensor_sub(sl, sl, R9c[:]),
                   reads=(f"R1_t{t}", "bcmr"), writes=(f"R1_t{t}",))
            if b == 0:
                mid_tasks.pop(12)
        if b == 0:
            mid_tasks.flush()

        P.claim("R2")
        P.claim("R3")
        P.claim("R4")
        W3O = NHT * SQ
        def ffn12_load(hc):
            par = hc % 2
            w1o = par * D
            w2o = 2 * D + par * D
            P.step("scalar", lambda e, hc=hc, w1o=w1o: e.dma_start(
                R3[:, w1o:w1o + D], w1q_d[hc]),
                reads=scratch_reads.get(f"w1q{hc}", (f"w1q{hc}",)),
                writes=(f"R3F_w1{par}",), sem="C", inc=16)
            P.step("scalar", lambda e, hc=hc, w2o=w2o: e.dma_start(
                R3[:, w2o:w2o + D], w2q_d[hc]),
                reads=scratch_reads.get(f"w2q{hc}", (f"w2q{hc}",)),
                writes=(f"R3F_w2{par}",), sem="C", inc=16)

        def ffn3_load(oc):
            yp = oc % 2
            w3o = W3O + yp * FH
            w3reads = sum((scratch_reads.get(f"w3q{oc}_{j}", (f"w3q{oc}_{j}",))
                           for j in range(W3C)), ())
            P.step("scalar", lambda e, oc=oc, w3o=w3o: e.dma_start(
                R2[:, w3o:w3o + FH], w3q_d[oc]),
                reads=w3reads, writes=(f"R2F_w3{yp}",), sem="C", inc=16)

        for sq in range(NSQ):
            if sq == 0:
                ffn12_load(0)
            for hc in range(NHT):
                par = hc % 2
                w1o = par * D
                w2o = 2 * D + par * D
                if hc + 1 < NHT:
                    if sq == 0 or hc >= 1:
                        ffn12_load(hc + 1)
                elif sq + 1 < NSQ:
                    ffn12_load(0)
                if hc == NHT - 2:
                    ffn3_load(0)
                ppair = ((PAs(0), PAs(1), "pa0", "pa1"),
                         (PAs(2), PAs(3), "pa2", "pa3"),
                         (PBs(0), PBs(1), "pb0", "pb1"))[hc % 3]
                ps1, ps2, pn1, pn2 = ppair
                sgs = 3 * SQ + (hc % 2) * SQ
                sgn = f"R4_sg{hc % 2}"
                for wn, wo_, ps_, pn_ in ((f"R3F_w1{par}", w1o, ps1, pn1),
                                          (f"R3F_w2{par}", w2o, ps2, pn2)):
                    fns = []
                    for d in range(NT):
                        fns.append(lambda e, d=d, wo_=wo_, ps_=ps_, sq=sq: e.matmul(
                            ps_, R3[:, wo_ + d * 128: wo_ + (d + 1) * 128],
                            R1[:, d * S + sq * SQ: d * S + sq * SQ + SQ],
                            start=(d == 0), stop=(d == NT - 1)))
                    P.step("tensor", fns, reads=(wn,) + xn_names, writes=(pn_,))
                P.step("scalar",
                       lambda e, ps1=ps1, sgs=sgs: e.activation(
                           R4[:, sgs:sgs + SQ], ps1, AF.Sigmoid,
                           bias=0.0, scale=1.0),
                       reads=(pn1,), writes=(sgn,))
                P.step("vector",
                       lambda e, hc=hc, ps2=ps2, sgs=sgs: e.tensor_mul(
                           R2[:, hc * SQ:(hc + 1) * SQ], R4[:, sgs:sgs + SQ], ps2),
                       reads=(sgn, pn2), writes=(f"R2s{hc}",))
                if b == 0 and sq == 0:
                    ffn_tasks.pop(5)
            if b == 0:
                ffn_tasks.flush()
            for oc in range(NT):
                yp = oc % 2
                w3o = W3O + yp * FH
                if oc + 1 < NT:
                    ffn3_load(oc + 1)
                if oc == 2 and sq + 1 < NSQ:
                    ffn12_load(1)
                fns = []
                for ht in range(NHT):
                    fns.append(lambda e, ht=ht, yp=yp, w3o=w3o: e.matmul(
                        PBs(yp), R2[:, w3o + ht * 128: w3o + (ht + 1) * 128],
                        R2[:, ht * SQ:(ht + 1) * SQ],
                        start=(ht == 0), stop=(ht == NHT - 1)))
                P.step("tensor", fns, reads=(f"R2F_w3{yp}",) + hid_names, writes=(f"pb{yp}",))
                yofs = SQ + yp * SQ
                P.step("scalar",
                       lambda e, yofs=yofs, yp=yp: e.activation(
                           R4[:, yofs:yofs + SQ], PBs(yp), AF.Copy, bias=0.0, scale=1.0),
                       reads=(f"pb{yp}",), writes=(f"R4_y{yp}",))
                P.step("sync",
                       lambda e, yofs=yofs, oc=oc, sq=sq: e.dma_start(
                           ysp_d[oc][:, sq * SQ:(sq + 1) * SQ], R4[:, yofs:yofs + SQ]),
                       reads=(f"R4_y{yp}",), writes=(f"ysp{oc}_{sq}",), sem="Q", inc=16)
                if debug:
                    P.step("gpsimd",
                           lambda e, yofs=yofs, oc=oc, sq=sq: e.dma_start(
                               yspdbg_d[oc][:, sq * SQ:(sq + 1) * SQ], R4[:, yofs:yofs + SQ]),
                           reads=(f"R4_y{yp}",), writes=(f"yspdbg{oc}_{sq}",), sem="G", inc=16)
                P.step("tensor",
                       lambda e, yofs=yofs, oc=oc: e.matmul(
                           PBs(2), ones_bf[:], R4[:, yofs:yofs + SQ],
                           start=(oc == 0), stop=(oc == NT - 1), skip_group_check=True),
                       reads=(f"R4_y{yp}", "ones_bf"), writes=("pb2",))
                P.step("scalar",
                       lambda e, yofs=yofs: e.activation(
                           R6[:, 0:512], R4[:, yofs:yofs + SQ], AF.Square,
                           bias=0.0, scale=1.0),
                       reads=(f"R4_y{yp}",), writes=("exp0",))
                P.step("tensor",
                       lambda e, oc=oc: e.matmul(
                           PBs(3), ones_bf[:], R6[:, 0:512],
                           start=(oc == 0), stop=(oc == NT - 1), skip_group_check=True),
                       reads=("exp0", "ones_bf"), writes=("pb3",))
            c0 = sq * SQ
            P.step("vector",
                   lambda e, c0=c0: e.tensor_scalar(
                       R9a[:, c0:c0 + SQ], PBs(3), 1.0 / D, None, ALU.mult),
                   reads=("pb3",), writes=("bcr",))
            P.step("vector",
                   lambda e, c0=c0: e.tensor_scalar(
                       R8b[:, c0:c0 + SQ], PBs(2), 1.0 / D, None, ALU.mult),
                   reads=("pb2",), writes=("R8b",))
            P.step("vector",
                   lambda e, c0=c0: e.tensor_mul(
                       R9b[:, 0:512], R8b[:, c0:c0 + SQ], R8b[:, c0:c0 + SQ]),
                   reads=("R8b",), writes=("R9b",))
            P.step("vector",
                   lambda e, c0=c0: e.tensor_sub(
                       R9a[:, c0:c0 + SQ], R9a[:, c0:c0 + SQ], R9b[:, 0:512]),
                   reads=("bcr", "R9b"), writes=("bcr",))
            P.step("scalar",
                   lambda e, c0=c0: e.activation(
                       R9b[:, 0:512], R9a[:, c0:c0 + SQ], AF.Ln, bias=epscol[:], scale=1.0),
                   reads=("bcr", "epscol"), writes=("R9b",))
            P.step("scalar",
                   lambda e, c0=c0: e.activation(
                       R9a[:, c0:c0 + SQ], R9b[:, 0:512], AF.Exp, bias=0.0, scale=-0.5),
                   reads=("R9b",), writes=("bcr",))
            P.step("vector",
                   lambda e, c0=c0: e.tensor_mul(
                       R9c[:, c0:c0 + SQ], R8b[:, c0:c0 + SQ], R9a[:, c0:c0 + SQ]),
                   reads=("R8b", "bcr"), writes=("bcmr",))

        if b == 0:
            ffn_tasks.sync_point()

        P.claim("R3")
        P.claim("R10")
        for t in range(NT):
            yofs = (t % 2) * S
            yn = f"R3_y{t % 2}"
            P.step("sync", lambda e, t=t, yofs=yofs: e.dma_start(
                R3[:, yofs:yofs + S], ysp_d[t]),
                reads=tuple(f"ysp{t}_{sq}" for sq in range(NSQ)),
                writes=(yn,), sem="Q", inc=16)
            P.step("sync", lambda e, t=t: e.dma_start(R8a[:], x1sp_d[t]),
                   reads=(f"x1sp{t}",), writes=("R8a",), sem="Q", inc=16)
            fw, fwn = (R10[:, 0:S], "R10") if t % 2 == 0 else (R8b[:], "R8b")
            P.step("gpsimd",
                   lambda e, fw=fw, yofs=yofs: e.tensor_mul(
                       fw[:], R3[:, yofs:yofs + S], R9a[:]),
                   reads=(yn, "bcr"), writes=(fwn,))
            P.step("vector", lambda e, fw=fw: e.tensor_sub(fw[:], fw[:], R9c[:]),
                   reads=(fwn, "bcmr"), writes=(fwn,))
            P.step("vector", lambda e, fw=fw: e.tensor_add(fw[:], fw[:], R8a[:]),
                   reads=(fwn, "R8a"), writes=(fwn,))
            P.step("sync", lambda e, fw=fw, t=t, b=b: e.dma_start(
                out_d[b][t * 128:(t + 1) * 128, :], fw[:]),
                reads=(fwn,), writes=(f"outd{b}_{t}",), sem="Q", inc=16)
        P.claim("R1_")
        P.claim("R2")
        P.claim("R4")

    assert att_tasks.h1 == len(att_tasks.tasks), "gamma tasks left"
    assert mid_tasks.h1 == len(mid_tasks.tasks), "w1/w2 quant tasks left"
    assert ffn_tasks.h1 == len(ffn_tasks.tasks), "w3 quant tasks left"

    for i in range(DMA_POOL):
        P.streams["sync"].append(("w", "Q" + str(i), P.cnt["Q" + str(i)]))
        P.streams["gpsimd"].append(("w", "G" + str(i), P.cnt["G" + str(i)]))
        P.streams["scalar"].append(("w", "C" + str(i), P.cnt["C" + str(i)]))

    with nc.Block() as block:
        def runner(eng_name):
            def go(e):
                for entry in P.streams[eng_name]:
                    if entry[0] == "w":
                        e.wait_ge(sems[entry[1]], entry[2])
                    else:
                        _, f, semname, inc = entry
                        inst = f(e)
                        if semname is not None:
                            inst.then_inc(sems[semname], inc)
            return go
        block.sync(runner("sync"))
        block.gpsimd(runner("gpsimd"))
        block.vector(runner("vector"))
        block.scalar(runner("scalar"))
        block.tensor(runner("tensor"))

    ctxmgr.close()
    return nc


def _perm_idx(H):
    idx = np.zeros(H * 128, np.int64)
    for h in range(H):
        for j in range(64):
            idx[h * 128 + j] = h * 128 + 2 * j
            idx[h * 128 + 64 + j] = h * 128 + 2 * j + 1
    return idx


def _pack(wT, NC, CC):
    Din, Dout = wT.shape
    NTl = Din // 128
    assert Dout == NC * CC
    a = wT.reshape(NTl, 128, NC, CC)
    return np.ascontiguousarray(a.transpose(2, 1, 0, 3).reshape(NC, 128, NTl * CC))


def _trig_table(S):
    j = np.arange(64, dtype=np.float64)
    theta = 10000.0 ** (-j / 64.0)
    pos = np.arange(S, dtype=np.float64)
    fr = pos[None, :] * theta[:, None]
    trig = np.zeros((128, S), np.float64)
    trig[0:64] = np.cos(fr)
    trig[64:128] = np.sin(fr)
    return trig.astype(np.float32)


def pack_core_inputs(cfg, x_core, wq, bq, wk, bk, wv, bv, wo, bo, w1, w2, w3):
    H, NT, NHT, NG, VG = cfg["H"], cfg["NT"], cfg["NHT"], cfg["NG"], cfg["VG"]
    S = cfg["S"]
    perm = _perm_idx(H)
    m = {
        "xt": x_core.transpose(0, 2, 1),
        "wq_p": _pack(np.ascontiguousarray(wq.T[:, perm]), H, 128),
        "wk_p": _pack(np.ascontiguousarray(wk.T[:, perm]), H, 128),
        "wv_p": _pack(np.ascontiguousarray(wv.T), H, 128),
        "wo_p": _pack(np.ascontiguousarray(wo.T), NT, 128),
        "w1_p": _pack(np.ascontiguousarray(w1.T), NHT, 128),
        "w2_p": _pack(np.ascontiguousarray(w2.T), NHT, 128),
        "w3_p": _pack(np.ascontiguousarray(w3.T), NT, 128),
        "bq_p": bq[perm].reshape(H, 128).T,
        "bk_p": bk[perm].reshape(H, 128).T,
        "bv_p": bv.reshape(H, 128).T,
        "bo_p": bo.reshape(NT, 128).T,
        "trig": _trig_table(S),
    }
    return {k: np.ascontiguousarray(v, dtype=np.float32) for k, v in m.items()}


def kernel(L0_attn_x, L0_attn_wq, L0_attn_bq, L0_attn_wk, L0_attn_bk,
           L0_attn_wv, L0_attn_bv, L0_attn_wo, L0_attn_bo,
           L0_ffn_w1, L0_ffn_w2, L0_ffn_w3):
    cfg = make_cfg()
    NCORES = 8
    NB = cfg["NB"]
    x = np.asarray(L0_attn_x, dtype=np.float32)
    wq, bq, wk, bk, wv, bv, wo, bo, w1, w2, w3 = [
        np.asarray(a, dtype=np.float32) for a in
        (L0_attn_wq, L0_attn_bq, L0_attn_wk, L0_attn_bk, L0_attn_wv,
         L0_attn_bv, L0_attn_wo, L0_attn_bo, L0_ffn_w1, L0_ffn_w2, L0_ffn_w3)]

    nc = build_program(cfg)
    in_maps = []
    for c in range(NCORES):
        xc = x[c * NB:(c + 1) * NB]
        in_maps.append(pack_core_inputs(cfg, xc, wq, bq, wk, bk, wv, bv,
                                        wo, bo, w1, w2, w3))
    res = run_bass_kernel_spmd(nc, in_maps, core_ids=list(range(NCORES)))
    outs = [r["out"] for r in res.results]
    full = np.concatenate(outs, axis=0)
    return np.ascontiguousarray(full.transpose(0, 2, 1))


# version 24 (final)
# speedup vs baseline: 1.0314x; 1.0314x over previous
import contextlib
import numpy as np

import concourse.bass as bass
import concourse.mybir as mybir
from concourse.bass_utils import run_bass_kernel_spmd

F32 = mybir.dt.float32
BF16 = mybir.dt.bfloat16
ALU = mybir.AluOpType
AF = mybir.ActivationFunctionType
AX = mybir.AxisListType.X

INV_SQRT_HD = 0.08838834764831845
C_SCALE = 1.02 * INV_SQRT_HD / 2.0
LN_EPS = 1e-5
Q_EPS = 1e-8


def make_cfg(D=2048, S=2048, H=16, FH=5632, NB=2):
    c = dict(D=D, S=S, H=H, FH=FH, NB=NB)
    c["NT"] = D // 128
    c["NHT"] = FH // 128
    c["NKT"] = S // 128
    c["NQC"] = S // 512
    c["VG"] = 2
    c["NG"] = H // 2
    c["SQ"] = 512
    c["NSQ"] = S // 512
    return c


ENGSEM = {"sync": "Q", "gpsimd": "P", "vector": "V", "scalar": "S", "tensor": "T"}


DMA_POOL = 8


class Prog:

    def __init__(self):
        self.streams = {e: [] for e in ("sync", "gpsimd", "vector", "scalar", "tensor")}
        self.cnt = {}
        self.semnames = []
        for k in "VSTP":
            self.cnt[k] = 0
            self.semnames.append(k)
        for q in "QGC":
            for i in range(DMA_POOL):
                self.cnt[q + str(i)] = 0
                self.semnames.append(q + str(i))
        self.qrot = {q: 0 for q in "QGC"}
        self.waited = {e: {} for e in self.streams}
        self.bufs = {}
        self.pending = {}

    def _buf(self, name):
        return self.bufs.setdefault(name, {"w": None, "r": {}})

    def wait(self, eng, tok):
        sem, val = tok
        if val <= 0:
            return
        w = self.waited[eng]
        if w.get(sem, 0) >= val:
            return
        w[sem] = val
        self.streams[eng].append(("w", sem, val))

    def step(self, eng, fns, reads=(), writes=(), sem=None, inc=1):
        if callable(fns):
            fns = [fns]
        for b in reads:
            st = self._buf(b)
            if st["w"]:
                self.wait(eng, st["w"])
        for b in writes:
            for pref, toks in self.pending.items():
                if b.startswith(pref):
                    for tok in toks:
                        self.wait(eng, tok)
            st = self._buf(b)
            if st["w"]:
                self.wait(eng, st["w"])
            for s, v in st["r"].items():
                self.wait(eng, (s, v))
        if sem is None:
            sem = ENGSEM[eng]
        if sem in ("Q", "G", "C"):
            i = self.qrot[sem]
            self.qrot[sem] = (i + 1) % DMA_POOL
            sem = sem + str(i)
            self.wait(eng, (sem, self.cnt[sem]))
        self.cnt[sem] += inc
        tok = (sem, self.cnt[sem])
        for f in fns[:-1]:
            self.streams[eng].append(("i", f, None, 0))
        self.streams[eng].append(("i", fns[-1], sem, inc))
        for b in reads:
            st = self._buf(b)
            st["r"][tok[0]] = max(st["r"].get(tok[0], 0), tok[1])
        for b in writes:
            self.bufs[b] = {"w": tok, "r": {}}
        return tok

    def claim(self, prefix):
        toks = {}
        for name in [n for n in self.bufs if n.startswith(prefix)]:
            st = self.bufs.pop(name)
            if st["w"]:
                s, v = st["w"]
                toks[s] = max(toks.get(s, 0), v)
            for s, v in st["r"].items():
                toks[s] = max(toks.get(s, 0), v)
        self.pending[prefix] = list(toks.items())


def build_program(cfg, debug=False):
    D, S, H, FH, NB = cfg["D"], cfg["S"], cfg["H"], cfg["FH"], cfg["NB"]
    NT, NHT, NKT, NQC = cfg["NT"], cfg["NHT"], cfg["NKT"], cfg["NQC"]
    VG, NG, SQ, NSQ = cfg["VG"], cfg["NG"], cfg["SQ"], cfg["NSQ"]
    PCW = min(2048, S)
    W3C = (FH + PCW - 1) // PCW
    assert S == D, "R3 role layout assumes S == D"

    nc = bass.Bass("TRN2", target_bir_lowering=False)

    xt_d = nc.dram_tensor("xt", [NB, D, S], F32, kind="ExternalInput").ap()
    wq_d = nc.dram_tensor("wq_p", [H, 128, D], F32, kind="ExternalInput").ap()
    wk_d = nc.dram_tensor("wk_p", [H, 128, D], F32, kind="ExternalInput").ap()
    wv_d = nc.dram_tensor("wv_p", [H, 128, D], F32, kind="ExternalInput").ap()
    wo_d = nc.dram_tensor("wo_p", [NT, 128, D], F32, kind="ExternalInput").ap()
    w1_d = nc.dram_tensor("w1_p", [NHT, 128, D], F32, kind="ExternalInput").ap()
    w2_d = nc.dram_tensor("w2_p", [NHT, 128, D], F32, kind="ExternalInput").ap()
    w3_d = nc.dram_tensor("w3_p", [NT, 128, FH], F32, kind="ExternalInput").ap()
    bq_d = nc.dram_tensor("bq_p", [128, H], F32, kind="ExternalInput").ap()
    bk_d = nc.dram_tensor("bk_p", [128, H], F32, kind="ExternalInput").ap()
    bv_d = nc.dram_tensor("bv_p", [128, H], F32, kind="ExternalInput").ap()
    bo_d = nc.dram_tensor("bo_p", [128, NT], F32, kind="ExternalInput").ap()
    trig_d = nc.dram_tensor("trig", [128, S], F32, kind="ExternalInput").ap()
    out_d = nc.dram_tensor("out", [NB, D, S], F32, kind="ExternalOutput").ap()
    w1q_d = nc.dram_tensor("w1q", [NHT, 128, D], BF16, kind="Internal").ap()
    w2q_d = nc.dram_tensor("w2q", [NHT, 128, D], BF16, kind="Internal").ap()
    w3q_d = nc.dram_tensor("w3q", [NT, 128, FH], BF16, kind="Internal").ap()
    x1sp_d = nc.dram_tensor("x1sp", [NT, 128, S], F32,
                            kind="ExternalOutput" if debug else "Internal").ap()
    osp_d = nc.dram_tensor("osp", [NT, 128, S], BF16, kind="Internal").ap()
    ysp_d = nc.dram_tensor("ysp", [NT, 128, S], BF16, kind="Internal").ap()
    if debug:
        ospdbg_d = nc.dram_tensor("ospdbg", [NT, 128, S], F32, kind="ExternalOutput").ap()
        yspdbg_d = nc.dram_tensor("yspdbg", [NT, 128, S], F32, kind="ExternalOutput").ap()
        w1qdbg_d = nc.dram_tensor("w1qdbg", [NHT, 128, D], F32, kind="ExternalOutput").ap()
        w2qdbg_d = nc.dram_tensor("w2qdbg", [NHT, 128, D], F32, kind="ExternalOutput").ap()
        w3qdbg_d = nc.dram_tensor("w3qdbg", [NT, 128, FH], F32, kind="ExternalOutput").ap()
        dbg_q = {"w1": w1qdbg_d, "w2": w2qdbg_d, "w3": w3qdbg_d}

    cur = [((nc.sbuf_base + 31) // 32 * 32)]
    limit = nc.SBUF_PARTITION_SIZE_BYTES - 32

    def carve(name, shape, dtype):
        h = nc.alloc_sbuf_tensor_at(name, shape, dtype, offset=cur[0])
        per = 1
        for s in shape[1:]:
            per *= s
        per *= mybir.dt.size(dtype)
        cur[0] += (per + 31) // 32 * 32
        assert cur[0] <= limit, f"SBUF overflow at {name}: {cur[0]} > {limit}"
        return h

    R1 = carve("R1", [128, NT * S], BF16)
    R2 = carve("R2", [128, max(NT * S, NHT * SQ + 2 * FH)], BF16)
    R3 = carve("R3", [128, 4 * D], BF16)
    R4 = carve("R4", [128, max(2 * S, 5 * SQ)], BF16)
    R5 = carve("R5", [128, S], BF16)
    S_R6 = max(2048, S)
    R6 = carve("R6", [128, S_R6], BF16)
    R8a = carve("R8a", [128, S], F32)
    R8b = carve("R8b", [128, S], F32)
    R9a = carve("R9a", [128, S], F32)
    R9b = carve("R9b", [128, 512], F32)
    R9c = carve("R9c", [128, S], BF16)
    R10 = carve("R10", [128, max(2048, S)], F32)
    trig_sb = carve("trig_sb", [128, S], BF16)
    ones_bf = carve("ones_bf", [128, 128], BF16)
    onesrow = carve("onesrow", [1, 128], F32)
    onescol = carve("onescol", [128, 1], F32)
    epscol = carve("epscol", [128, 1], F32)
    bq_sb = carve("bq_sb", [128, H], F32)
    bk_sb = carve("bk_sb", [128, H], F32)
    bv_sb = carve("bv_sb", [128, H], F32)
    bo_sb = carve("bo_sb", [128, NT], F32)
    gpart = carve("gpart", [128, 64], F32)
    smallc = carve("smallc", [128, 8], F32)
    smallr = carve("smallr", [1, 16], F32)

    P = Prog()
    P_SEMNAMES = P.semnames
    ctxmgr = contextlib.ExitStack()
    PA = ctxmgr.enter_context(nc.psum_tensor("PA", [128, 2048], F32))
    PB = ctxmgr.enter_context(nc.psum_tensor("PB", [128, 2048], F32))
    sems = {k: ctxmgr.enter_context(nc.semaphore("sem" + k)) for k in P_SEMNAMES}

    def PAs(i):
        return PA[:, i * 512:(i + 1) * 512]

    def PBs(i):
        return PB[:, i * 512:(i + 1) * 512]

    P.step("vector", lambda e: e.memset(ones_bf[:], 1.0), writes=("ones_bf",))
    P.step("vector", lambda e: e.memset(onesrow[:], 1.0), writes=("onesrow",))
    P.step("vector", lambda e: e.memset(onescol[:], 1.0), writes=("onescol",))
    P.step("vector", lambda e: e.memset(epscol[:], LN_EPS), writes=("epscol",))
    P.step("gpsimd", lambda e: e.dma_start(trig_sb[:], trig_d[:]), writes=("trig",), sem="G", inc=16)
    P.step("sync", lambda e: e.dma_start(bq_sb[:], bq_d[:]), writes=("bq",), sem="Q", inc=16)
    P.step("sync", lambda e: e.dma_start(bk_sb[:], bk_d[:]), writes=("bk",), sem="Q", inc=16)
    P.step("sync", lambda e: e.dma_start(bv_sb[:], bv_d[:]), writes=("bv",), sem="Q", inc=16)
    P.step("sync", lambda e: e.dma_start(bo_sb[:], bo_d[:]), writes=("bo",), sem="Q", inc=16)

    t1l = [(w1_d[i], D, w1q_d[i], f"w1q{i}") for i in range(NHT)]
    t2l = [(w2_d[i], D, w2q_d[i], f"w2q{i}") for i in range(NHT)]
    t3l = []
    for oc in range(NT):
        for j in range(W3C):
            c0, c1 = j * PCW, min((j + 1) * PCW, FH)
            t3l.append((w3_d[oc][:, c0:c1], c1 - c0, w3q_d[oc][:, c0:c1], f"w3q{oc}_{j}"))
    prep_mats = [("w1", t1l, NHT * 128 * D), ("w2", t2l, NHT * 128 * D),
                 ("w3", t3l, NT * 128 * FH)]
    scratch_reads = {}

    def quarters(ncols):
        qs = []
        c = 0
        while c < ncols:
            qs.append((c, min(ncols, c + 512)))
            c += 512
        return qs

    def gamma_quarter_tasks(tile_idx, gq0, tile):
        src, ncols, _, _ = tile
        qs = quarters(ncols)
        tasks = []
        nq = len(qs)
        for qi, (c0, c1) in enumerate(qs):
            slot = 1 + (gq0 + qi) % 3
            stg = R10[:, slot * 512: slot * 512 + (c1 - c0)]
            stn = f"R10q{slot}"

            def p0(src=src, c0=c0, c1=c1, stg=stg, stn=stn):
                P.step("sync", lambda e: e.dma_start(stg, src[:, c0:c1]),
                       writes=(stn,), sem="Q", inc=16)

            def p1(qi=qi, stg=stg, stn=stn, tile_idx=tile_idx,
                   last=(qi == len(qs) - 1), nq=nq):
                P.step("vector",
                       lambda e: e.tensor_reduce(gpart[:, 56 + qi % 4: 57 + qi % 4],
                                                 stg, AX, ALU.add,
                                                 apply_absolute_value=True),
                       reads=(stn,), writes=("gpartq",))
                if last:
                    P.step("vector",
                           lambda e: e.tensor_reduce(
                               gpart[:, tile_idx:tile_idx + 1],
                               gpart[:, 56:56 + nq], AX, ALU.add),
                           reads=("gpartq",), writes=("gpart",))
            tasks.append((p0, p1))
        return tasks

    def gamma_fin_task(mi, ntiles, nelem):
        c4, c5, c6 = 4 + 3 * mi, 5 + 3 * mi, 6 + 3 * mi

        def p1():
            P.step("vector",
                   lambda e: e.tensor_reduce(gpart[:, 60:61], gpart[:, 0:ntiles], AX, ALU.add),
                   reads=("gpart",), writes=("gpart",))
            P.step("tensor",
                   lambda e: e.matmul(PBs(0)[0:1, 0:1], onescol[:], gpart[:, 60:61],
                                      start=True, stop=True),
                   reads=("gpart", "onescol"), writes=("pb0",))
            P.step("scalar", lambda e: e.copy(smallr[0:1, c4:c4 + 1], PBs(0)[0:1, 0:1]),
                   reads=("pb0",), writes=(f"gsum{mi}",))
            P.step("vector",
                   lambda e: e.tensor_scalar(smallr[0:1, c5:c5 + 1], smallr[0:1, c4:c4 + 1],
                                             1.0 / nelem, Q_EPS, ALU.mult, ALU.add),
                   reads=(f"gsum{mi}",), writes=(f"gden{mi}",))
            P.step("vector", lambda e: e.reciprocal(smallr[0:1, c6:c6 + 1], smallr[0:1, c5:c5 + 1]),
                   reads=(f"gden{mi}",), writes=(f"grec{mi}",))
            P.step("tensor",
                   lambda e: e.matmul(PBs(0)[:, 0:1], onesrow[:], smallr[0:1, c6:c6 + 1],
                                      start=True, stop=True),
                   reads=(f"grec{mi}", "onesrow"), writes=("pb0",))
            P.step("scalar", lambda e: e.copy(smallc[:, 1 + mi:2 + mi], PBs(0)[:, 0:1]),
                   reads=("pb0",), writes=(f"rgcol{mi}",))
        return (lambda: None, p1)

    def quant_quarter_tasks(mi, qidx0, tile, dbg_dst=None):
        src, ncols, dst, bufname = tile
        qs = quarters(ncols)
        names = []
        tasks = []
        for qi, (c0, c1) in enumerate(qs):
            slot = (qidx0 + qi) % 4
            w = c1 - c0
            stg = R10[:, slot * 512: slot * 512 + w]
            stn = f"R10q{slot}"
            qn = f"{bufname}q{qi}"
            names.append(qn)

            def p0(src=src, c0=c0, c1=c1, stg=stg, stn=stn):
                P.step("sync", lambda e: e.dma_start(stg, src[:, c0:c1]),
                       writes=(stn,), sem="Q", inc=16)

            def p1(c0=c0, c1=c1, w=w, stg=stg, stn=stn, qn=qn, dbg_dst=dbg_dst):
                P.step("vector",
                       lambda e: e.tensor_scalar(
                           R9b[:, 0:w], stg, smallc[:, 1 + mi:2 + mi], -0.5,
                           ALU.mult, ALU.is_le),
                       reads=(stn, f"rgcol{mi}"), writes=("R9b",))
                P.step("vector",
                       lambda e: e.tensor_scalar(
                           stg, stg, smallc[:, 1 + mi:2 + mi], 0.5,
                           ALU.mult, ALU.is_ge),
                       reads=(stn, f"rgcol{mi}"), writes=(stn,))
                P.step("vector",
                       lambda e: e.tensor_sub(stg, stg, R9b[:, 0:w]),
                       reads=(stn, "R9b"), writes=(stn,))
                P.step("gpsimd", lambda e: e.dma_start(dst[:, c0:c1], stg),
                       reads=(stn,), writes=(qn,), sem="G", inc=16)
                if dbg_dst is not None:
                    P.step("sync", lambda e: e.dma_start(dbg_dst[:, c0:c1], stg),
                           reads=(stn,), writes=(qn + "dbg",), sem="Q", inc=16)
            tasks.append((p0, p1))
        scratch_reads.setdefault(bufname, tuple(names))
        return tasks

    class SplitTasks:
        def __init__(self, tasks, look=3):
            self.tasks = tasks
            self.look = look
            self.h0 = 0
            self.h1 = 0

        def pop(self, n):
            for _ in range(n):
                while self.h0 < len(self.tasks) and self.h0 < self.h1 + self.look:
                    self.tasks[self.h0][0]()
                    self.h0 += 1
                if self.h1 < len(self.tasks):
                    self.tasks[self.h1][1]()
                    self.h1 += 1

        def flush(self):
            self.pop(len(self.tasks))

        def sync_point(self):
            while self.h1 < self.h0:
                self.tasks[self.h1][1]()
                self.h1 += 1

    gl = []
    quants = {}
    qct = 0
    gqc = 0
    for mi, (nm, tiles, nelem) in enumerate(prep_mats):
        for i, tile in enumerate(tiles):
            gl.extend(gamma_quarter_tasks(i, gqc, tile))
            gqc += len(quarters(tile[1]))
        gl.append(gamma_fin_task(mi, len(tiles), nelem))
        qs = []
        for ti, tile in enumerate(tiles):
            dd = None
            if debug:
                src, ncols, dst, bufname = tile
                if nm in ("w1", "w2"):
                    dd = dbg_q[nm][ti]
                else:
                    oc, j = ti // W3C, ti % W3C
                    dd = dbg_q["w3"][oc][:, j * PCW:j * PCW + ncols]
            qs.extend(quant_quarter_tasks(mi, qct, tile, dd))
            qct += len(quarters(tile[1]))
        quants[nm] = qs
    att_tasks = SplitTasks(gl, look=2)
    mid_tasks = SplitTasks(quants.get("w1", []) + quants.get("w2", []), look=3)
    ffn_tasks = SplitTasks(quants.get("w3", []), look=3)

    def ln_stats_pipeline(n_tiles, srcgen):
        exp_all = tuple(f"exp{i}" for i in range(4))
        pend = None
        for t in range(n_tiles):
            src_ap, src_reads, is_f32 = srcgen(t)
            if t % 2 == 0:
                cb, cbn, sqb, sqn = R5[:, 0:S], ("R5full",), R6[:, 0:S], exp_all
            else:
                cb, cbn, sqb, sqn = R4[:, 0:S], ("R4a",), R4[:, S:2 * S], ("R4b",)
            if is_f32:
                P.step("vector", lambda e, cb=cb, src_ap=src_ap: e.tensor_copy(cb, src_ap),
                       reads=tuple(src_reads), writes=cbn)
                sum_ap, sum_reads = cb, cbn
            else:
                sum_ap, sum_reads = src_ap, tuple(src_reads)
            P.step("scalar", lambda e, sqb=sqb, src_ap=src_ap: e.activation(
                sqb, src_ap, AF.Square, bias=0.0, scale=1.0),
                reads=tuple(src_reads), writes=sqn)
            if pend is not None:
                pend()

            def mk(t=t, sum_ap=sum_ap, sum_reads=sum_reads, sqb=sqb, sqn=sqn):
                def go():
                    for qc in range(NQC):
                        P.step("tensor",
                               lambda e, qc=qc: e.matmul(
                                   PAs(qc), ones_bf[:], sum_ap[:, qc * 512:(qc + 1) * 512],
                                   start=(t == 0), stop=(t == n_tiles - 1),
                                   skip_group_check=True),
                               reads=("ones_bf",) + tuple(sum_reads), writes=(f"pa{qc}",))
                        P.step("tensor",
                               lambda e, qc=qc: e.matmul(
                                   PBs(qc), ones_bf[:], sqb[:, qc * 512:(qc + 1) * 512],
                                   start=(t == 0), stop=(t == n_tiles - 1),
                                   skip_group_check=True),
                               reads=("ones_bf",) + tuple(sqn), writes=(f"pb{qc}",))
                return go
            pend = mk()
        pend()

    def ln_rows():
        for qc in range(NQC):
            c0, c1 = qc * 512, (qc + 1) * 512
            P.step("vector",
                   lambda e, c0=c0, c1=c1, qc=qc: e.tensor_scalar(
                       R9a[:, c0:c1], PBs(qc), 1.0 / D, None, ALU.mult),
                   reads=(f"pb{qc}",), writes=("bcr",))
            P.step("vector",
                   lambda e, c0=c0, c1=c1, qc=qc: e.tensor_scalar(
                       R8b[:, c0:c1], PAs(qc), 1.0 / D, None, ALU.mult),
                   reads=(f"pa{qc}",), writes=("R8b",))
            P.step("vector",
                   lambda e, c0=c0, c1=c1: e.tensor_mul(
                       R9b[:, 0:512], R8b[:, c0:c1], R8b[:, c0:c1]),
                   reads=("R8b",), writes=("R9b",))
            P.step("vector",
                   lambda e, c0=c0, c1=c1: e.tensor_sub(
                       R9a[:, c0:c1], R9a[:, c0:c1], R9b[:, 0:512]),
                   reads=("bcr", "R9b"), writes=("bcr",))
            P.step("scalar",
                   lambda e, c0=c0, c1=c1: e.activation(
                       R9b[:, 0:512], R9a[:, c0:c1], AF.Ln, bias=epscol[:], scale=1.0),
                   reads=("bcr", "epscol"), writes=("R9b",))
            P.step("scalar",
                   lambda e, c0=c0, c1=c1: e.activation(
                       R9a[:, c0:c1], R9b[:, 0:512], AF.Exp, bias=0.0, scale=-0.5),
                   reads=("R9b",), writes=("bcr",))
            P.step("vector",
                   lambda e, c0=c0, c1=c1: e.tensor_mul(
                       R9c[:, c0:c1], R8b[:, c0:c1], R9a[:, c0:c1]),
                   reads=("R8b", "bcr"), writes=("bcmr",))

    xn_names = tuple(f"R1_t{t}" for t in range(NT))
    ctx_names = tuple(f"R2_h{h}" for h in range(H))
    hid_names = tuple(f"R2s{hc}" for hc in range(NHT))
    vtok_names = tuple(f"vtok{tt}" for tt in range(NKT))

    for b in range(NB):
        P.claim("R4")
        P.claim("R5")

        def l1_src(t):
            x, xn = (R8a, "R8a") if t % 2 == 0 else (R8b, "R8b")
            P.step("sync", lambda e, t=t, x=x, b=b: e.dma_start(
                x[:], xt_d[b][t * 128:(t + 1) * 128, :]),
                writes=(xn,), sem="Q", inc=16)
            return x[:], (xn,), True
        ln_stats_pipeline(NT, l1_src)
        ln_rows()
        for t in range(NT):
            x, xn = (R8a, "R8a") if t % 2 == 0 else (R8b, "R8b")
            P.step("sync", lambda e, t=t, x=x, b=b: e.dma_start(
                x[:], xt_d[b][t * 128:(t + 1) * 128, :]),
                writes=(xn,), sem="Q", inc=16)
            sl = R1[:, t * S:(t + 1) * S]
            P.step("vector", lambda e, sl=sl, x=x: e.tensor_mul(sl, x[:], R9a[:]),
                   reads=(xn, "bcr"), writes=(f"R1_t{t}",))
            P.step("vector", lambda e, sl=sl: e.tensor_sub(sl, sl, R9c[:]),
                   reads=(f"R1_t{t}", "bcmr"), writes=(f"R1_t{t}",))

        P.claim("R3")
        P.claim("R4")
        P.claim("R5")
        def emit_wload(h):
            P.step("gpsimd", lambda e, h=h: e.dma_start(R3[:, 0:D], wq_d[h]),
                   writes=("R3_wq",), sem="G", inc=16)
            P.step("gpsimd", lambda e, h=h: e.dma_start(R3[:, D:2 * D], wk_d[h]),
                   writes=("R3_wk",), sem="G", inc=16)
            P.step("gpsimd", lambda e, h=h: e.dma_start(R3[:, 2 * D:3 * D], wv_d[h]),
                   writes=("R3_wv",), sem="G", inc=16)

        emit_wload(0)
        prev_chain = None
        chain_defer_qc = 1 if ((NQC - 1) % 2 == 1 and NQC >= 4) else None
        for h in range(H):
            if chain_defer_qc is None and prev_chain is not None:
                prev_chain()
                prev_chain = None
                if b == 0:
                    att_tasks.pop(5)
            qk_group_i = 0
            for wbufn, wofs, PS, PSfull, pnm, qkT, qkn, bsb, bn in (
                    ("R3_wq", 0, PAs, PA, "pa", R4[:, 0:S], "R4_q", bq_sb, "bq"),
                    ("R3_wk", D, PBs, PB, "pb", R4[:, S:2 * S], "R4_k", bk_sb, "bk")):
                for qc in range(NQC):
                    fns = []
                    for d in range(NT):
                        fns.append(lambda e, d=d, qc=qc, PS=PS, wofs=wofs: e.matmul(
                            PS(qc),
                            R3[:, wofs + d * 128: wofs + (d + 1) * 128],
                            R1[:, d * S + qc * 512: d * S + (qc + 1) * 512],
                            start=(d == 0), stop=(d == NT - 1)))
                    P.step("tensor", fns, reads=(wbufn,) + xn_names,
                           writes=(f"{pnm}{qc}",))
                    if qk_group_i == chain_defer_qc and prev_chain is not None:
                        prev_chain()
                        prev_chain = None
                        if b == 0:
                            att_tasks.pop(5)
                    qk_group_i += 1
                P.step("scalar",
                       lambda e, qkT=qkT, PSfull=PSfull, bsb=bsb, h=h: e.activation(
                           qkT, PSfull[:, 0:NQC * 512], AF.Identity,
                           bias=bsb[:, h:h + 1], scale=1.0),
                       reads=tuple(f"{pnm}{qc}" for qc in range(NQC)) + (bn,),
                       writes=(qkn,))
                rsc = R2[0:64, h * S:(h + 1) * S]
                P.step("vector",
                       lambda e, qkT=qkT, rsc=rsc: e.tensor_mul(
                           rsc, qkT[64:128, :], trig_sb[64:128, :]),
                       reads=(qkn, "trig"), writes=(f"R2_h{h}",))
                P.step("vector",
                       lambda e, qkT=qkT: e.tensor_mul(
                           qkT[0:64, :], qkT[0:64, :], trig_sb[0:64, :]),
                       reads=(qkn, "trig"), writes=(qkn,))
                P.step("vector",
                       lambda e, qkT=qkT, rsc=rsc: e.tensor_sub(
                           qkT[0:64, :], qkT[0:64, :], rsc),
                       reads=(qkn, f"R2_h{h}"), writes=(qkn,))

            for qkT, qkn, PS, PSfull, pnm, ridx in (
                    (R4[:, 0:S], "R4_q", PAs, PA, "pa", 0),
                    (R4[:, S:2 * S], "R4_k", PBs, PB, "pb", 1)):
                for qc in range(NQC):
                    slot = qc % 2
                    P.step("scalar",
                           lambda e, qc=qc, slot=slot, qkT=qkT: e.activation(
                               R6[:, slot * 512:(slot + 1) * 512],
                               qkT[:, qc * 512:(qc + 1) * 512], AF.Square,
                               bias=0.0, scale=1.0),
                           reads=(qkn,), writes=(f"exp{slot}",))
                    P.step("tensor",
                           lambda e, qc=qc, slot=slot, PS=PS: e.matmul(
                               PS(qc)[0:1, :], ones_bf[:, 0:1],
                               R6[:, slot * 512:(slot + 1) * 512],
                               start=True, stop=True),
                           reads=(f"exp{slot}", "ones_bf"), writes=(f"{pnm}{qc}",))
                P.step("vector",
                       lambda e, PSfull=PSfull, ridx=ridx: e.reduce_max(
                           smallr[0:1, ridx:ridx + 1], PSfull[0:1, 0:NQC * 512], AX),
                       reads=tuple(f"{pnm}{qc}" for qc in range(NQC)),
                       writes=(f"mx{ridx}",))
            P.step("vector",
                   lambda e: e.tensor_add(smallr[0:1, 2:3], smallr[0:1, 0:1],
                                          smallr[0:1, 1:2]),
                   reads=("mx0", "mx1"), writes=("c0",))
            P.step("vector",
                   lambda e: e.tensor_scalar(smallr[0:1, 3:4], smallr[0:1, 2:3],
                                             -C_SCALE, -1.0, ALU.mult, ALU.add),
                   reads=("c0",), writes=("c1",))

            for tt in range(NKT):
                ps = PAs(3) if tt % 2 == 0 else PBs(3)
                pn = "pa3" if tt % 2 == 0 else "pb3"
                fns = []
                for d in range(NT):
                    fns.append(lambda e, d=d, tt=tt, ps=ps: e.matmul(
                        ps[:, 0:128],
                        R1[:, d * S + tt * 128: d * S + tt * 128 + 128],
                        R3[:, 2 * D + d * 128: 2 * D + (d + 1) * 128],
                        start=(d == 0), stop=(d == NT - 1)))
                P.step("tensor", fns, reads=("R3_wv",) + xn_names, writes=(pn,))
                P.step("scalar",
                       lambda e, tt=tt, ps=ps: e.activation(
                           R5[:, tt * 128:(tt + 1) * 128], ps[:, 0:128],
                           AF.Copy, bias=0.0, scale=1.0),
                       reads=(pn,), writes=(f"R5v{tt}",))
            if h + 1 < H:
                emit_wload(h + 1)

            P.step("tensor",
                   lambda e: e.matmul(PAs(0)[:, 0:1], onesrow[:], smallr[0:1, 3:4],
                                      start=True, stop=True),
                   reads=("c1", "onesrow"), writes=("pa0",))
            P.step("scalar", lambda e: e.copy(smallc[:, 0:1], PAs(0)[:, 0:1]),
                   reads=("pa0",), writes=("cbias",))

            def make_chain(qc, cpn, cps, h=h):
                sl = R2[:, h * S + qc * 512: h * S + (qc + 1) * 512]

                def chain():
                    P.step("tensor",
                           lambda e: e.matmul(PAs(2), onesrow[:], R10[0:1, 0:512],
                                              start=True, stop=True),
                           reads=("R10", "onesrow"), writes=("pa2",))
                    P.step("scalar", lambda e: e.copy(R10[:, 0:512], PAs(2)),
                           reads=("pa2",), writes=("R10",))
                    P.step("vector", lambda e, cps=cps: e.tensor_mul(sl, cps, R10[:, 0:512]),
                           reads=(cpn, "R10"), writes=(f"R2_h{h}",))
                    P.step("vector",
                           lambda e: e.tensor_scalar(
                               sl, sl, bv_sb[:, h:h + 1], None, ALU.add),
                           reads=(f"R2_h{h}", "bv"), writes=(f"R2_h{h}",))
                return chain

            LOOKAHEAD = 3
            for qc in range(NQC):
                cpn = "pa0" if qc % 2 == 0 else "pa3"
                cps = PAs(0) if qc % 2 == 0 else PAs(3)
                dps = PAs(1) if qc % 2 == 0 else PBs(3)
                dpn = "pa1" if qc % 2 == 0 else "pb3"
                for step in range(NKT + LOOKAHEAD):
                    if step < NKT:
                        kt, pb, slot = step, step % 3, step % 4
                        P.step("tensor",
                               lambda e, kt=kt, qc=qc, pb=pb: e.matmul(
                                   PBs(pb), R4[:, S + kt * 128: S + (kt + 1) * 128],
                                   R4[:, qc * 512:(qc + 1) * 512],
                                   start=True, stop=True),
                               reads=("R4_q", "R4_k"), writes=(f"pb{pb}",))
                        P.step("scalar",
                               lambda e, pb=pb, slot=slot: e.activation(
                                   R6[:, slot * 512:(slot + 1) * 512], PBs(pb), AF.Exp,
                                   bias=smallc[:, 0:1], scale=INV_SQRT_HD),
                               reads=(f"pb{pb}", "cbias"), writes=(f"exp{slot}",))
                    if step >= LOOKAHEAD:
                        kt = step - LOOKAHEAD
                        slot = kt % 4
                        P.step("tensor",
                               lambda e, kt=kt, slot=slot, dps=dps: e.matmul(
                                   dps[0:1, :], ones_bf[:, 0:1],
                                   R6[:, slot * 512:(slot + 1) * 512],
                                   start=(kt == 0), stop=(kt == NKT - 1),
                                   skip_group_check=True),
                               reads=(f"exp{slot}", "ones_bf"), writes=(dpn,))
                        P.step("tensor",
                               lambda e, kt=kt, slot=slot, cps=cps: e.matmul(
                                   cps, R5[:, kt * 128:(kt + 1) * 128],
                                   R6[:, slot * 512:(slot + 1) * 512],
                                   start=(kt == 0), stop=(kt == NKT - 1),
                                   skip_group_check=True),
                               reads=(f"exp{slot}", f"R5v{kt}"), writes=(cpn,))
                        if kt == 1 and prev_chain is not None:
                            prev_chain()
                            prev_chain = None
                            if b == 0:
                                att_tasks.pop(5)
                P.step("vector", lambda e, dps=dps: e.reciprocal(R10[0:1, 0:512], dps[0:1, :]),
                       reads=(dpn,), writes=("R10",))
                prev_chain = make_chain(qc, cpn, cps)
        prev_chain()
        prev_chain = None

        if b == 0:
            att_tasks.flush()

        P.claim("R3")
        P.claim("R10")
        def wo_load(oc):
            ws = oc % 2
            wofs = ws * D
            P.step("gpsimd", lambda e, oc=oc, wofs=wofs: e.dma_start(
                R3[:, wofs:wofs + D], wo_d[oc]),
                writes=(f"R3_wo{ws}",), sem="G", inc=16)

        wo_load(0)
        for oc in range(NT):
            ws = oc % 2
            wofs = ws * D
            if oc + 1 < NT:
                wo_load(oc + 1)
            PS, PSfull, pnm = (PAs, PA, "pa") if oc % 2 == 0 else (PBs, PB, "pb")
            for qc in range(NQC):
                fns = []
                for d in range(NT):
                    fns.append(lambda e, d=d, qc=qc, PS=PS, wofs=wofs: e.matmul(
                        PS(qc), R3[:, wofs + d * 128: wofs + (d + 1) * 128],
                        R2[:, d * S + qc * 512: d * S + (qc + 1) * 512],
                        start=(d == 0), stop=(d == NT - 1)))
                P.step("tensor", fns, reads=(f"R3_wo{ws}",) + ctx_names,
                       writes=(f"{pnm}{qc}",))
            oofs = 2 * D + ws * D
            P.step("scalar",
                   lambda e, oofs=oofs, PSfull=PSfull, oc=oc: e.activation(
                       R3[:, oofs:oofs + S], PSfull[:, 0:NQC * 512],
                       AF.Identity, bias=bo_sb[:, oc:oc + 1], scale=1.0),
                   reads=tuple(f"{pnm}{qc}" for qc in range(NQC)) + ("bo",),
                   writes=(f"R3_os{ws}",))
            P.step("sync", lambda e, oofs=oofs, oc=oc: e.dma_start(
                osp_d[oc], R3[:, oofs:oofs + S]),
                reads=(f"R3_os{ws}",), writes=(f"osp{oc}",), sem="Q", inc=16)
            if b == 0:
                mid_tasks.pop(8)
            if debug:
                P.step("gpsimd", lambda e, oofs=oofs, oc=oc: e.dma_start(
                    ospdbg_d[oc], R3[:, oofs:oofs + S]),
                    reads=(f"R3_os{ws}",), writes=(f"ospdbg{oc}",), sem="G", inc=16)

        if b == 0:
            mid_tasks.sync_point()

        P.claim("R4")
        P.claim("R5")

        def ln2_src(t):
            oofs = 2 * D + (t % 2) * D
            osn = f"R3_os{t % 2}"
            P.step("sync", lambda e, t=t, oofs=oofs: e.dma_start(
                R3[:, oofs:oofs + S], osp_d[t]),
                reads=(f"osp{t}",), writes=(osn,), sem="Q", inc=16)
            return R3[:, oofs:oofs + S], (osn,), False
        ln_stats_pipeline(NT, ln2_src)
        ln_rows()

        P.claim("R10")

        def res1_src(t):
            oofs = 2 * D + (t % 2) * D
            osn = f"R3_os{t % 2}"
            P.step("sync", lambda e, t=t, oofs=oofs: e.dma_start(
                R3[:, oofs:oofs + S], osp_d[t]),
                reads=(f"osp{t}",), writes=(osn,), sem="Q", inc=16)
            P.step("sync", lambda e, t=t, b=b: e.dma_start(
                R8a[:], xt_d[b][t * 128:(t + 1) * 128, :]),
                writes=("R8a",), sem="Q", inc=16)
            fw, fwn = (R10[:, 0:S], "R10") if t % 2 == 0 else (R8b[:], "R8b")
            P.step("gpsimd",
                   lambda e, fw=fw, oofs=oofs: e.tensor_mul(
                       fw[:], R3[:, oofs:oofs + S], R9a[:]),
                   reads=(osn, "bcr"), writes=(fwn,))
            P.step("vector", lambda e, fw=fw: e.tensor_sub(fw[:], fw[:], R9c[:]),
                   reads=(fwn, "bcmr"), writes=(fwn,))
            P.step("vector", lambda e, fw=fw: e.tensor_add(fw[:], fw[:], R8a[:]),
                   reads=(fwn, "R8a"), writes=(fwn,))
            P.step("sync", lambda e, fw=fw, t=t: e.dma_start(x1sp_d[t], fw[:]),
                   reads=(fwn,), writes=(f"x1sp{t}",), sem="Q", inc=16)
            return fw[:], (fwn,), True
        ln_stats_pipeline(NT, res1_src)
        ln_rows()

        P.claim("R10")
        for t in range(NT):
            x, xn = (R8a, "R8a") if t % 2 == 0 else (R8b, "R8b")
            P.step("sync", lambda e, t=t, x=x: e.dma_start(x[:], x1sp_d[t]),
                   reads=(f"x1sp{t}",), writes=(xn,), sem="Q", inc=16)
            sl = R1[:, t * S:(t + 1) * S]
            P.step("gpsimd", lambda e, sl=sl, x=x: e.tensor_mul(sl, x[:], R9a[:]),
                   reads=(xn, "bcr"), writes=(f"R1_t{t}",))
            P.step("vector", lambda e, sl=sl: e.tensor_sub(sl, sl, R9c[:]),
                   reads=(f"R1_t{t}", "bcmr"), writes=(f"R1_t{t}",))
            if b == 0:
                mid_tasks.pop(8)
        if b == 0:
            mid_tasks.flush()

        P.claim("R2")
        P.claim("R3")
        P.claim("R4")
        W3O = NHT * SQ
        def ffn12_load(hc):
            par = hc % 2
            w1o = par * D
            w2o = 2 * D + par * D
            P.step("scalar", lambda e, hc=hc, w1o=w1o: e.dma_start(
                R3[:, w1o:w1o + D], w1q_d[hc]),
                reads=scratch_reads.get(f"w1q{hc}", (f"w1q{hc}",)),
                writes=(f"R3F_w1{par}",), sem="C", inc=16)
            P.step("scalar", lambda e, hc=hc, w2o=w2o: e.dma_start(
                R3[:, w2o:w2o + D], w2q_d[hc]),
                reads=scratch_reads.get(f"w2q{hc}", (f"w2q{hc}",)),
                writes=(f"R3F_w2{par}",), sem="C", inc=16)

        def ffn3_load(oc):
            yp = oc % 2
            w3o = W3O + yp * FH
            w3reads = sum((scratch_reads.get(f"w3q{oc}_{j}", (f"w3q{oc}_{j}",))
                           for j in range(W3C)), ())
            P.step("scalar", lambda e, oc=oc, w3o=w3o: e.dma_start(
                R2[:, w3o:w3o + FH], w3q_d[oc]),
                reads=w3reads, writes=(f"R2F_w3{yp}",), sem="C", inc=16)

        for sq in range(NSQ):
            if sq == 0:
                ffn12_load(0)
            for hc in range(NHT):
                par = hc % 2
                w1o = par * D
                w2o = 2 * D + par * D
                if hc + 1 < NHT:
                    if sq == 0 or hc >= 1:
                        ffn12_load(hc + 1)
                elif sq + 1 < NSQ:
                    ffn12_load(0)
                if hc == NHT - 2:
                    ffn3_load(0)
                ppair = ((PAs(0), PAs(1), "pa0", "pa1"),
                         (PAs(2), PAs(3), "pa2", "pa3"),
                         (PBs(0), PBs(1), "pb0", "pb1"))[hc % 3]
                ps1, ps2, pn1, pn2 = ppair
                sgs = 3 * SQ + (hc % 2) * SQ
                sgn = f"R4_sg{hc % 2}"
                for wn, wo_, ps_, pn_ in ((f"R3F_w1{par}", w1o, ps1, pn1),
                                          (f"R3F_w2{par}", w2o, ps2, pn2)):
                    fns = []
                    for d in range(NT):
                        fns.append(lambda e, d=d, wo_=wo_, ps_=ps_, sq=sq: e.matmul(
                            ps_, R3[:, wo_ + d * 128: wo_ + (d + 1) * 128],
                            R1[:, d * S + sq * SQ: d * S + sq * SQ + SQ],
                            start=(d == 0), stop=(d == NT - 1)))
                    P.step("tensor", fns, reads=(wn,) + xn_names, writes=(pn_,))
                P.step("scalar",
                       lambda e, ps1=ps1, sgs=sgs: e.activation(
                           R4[:, sgs:sgs + SQ], ps1, AF.Sigmoid,
                           bias=0.0, scale=1.0),
                       reads=(pn1,), writes=(sgn,))
                P.step("vector",
                       lambda e, hc=hc, ps2=ps2, sgs=sgs: e.tensor_mul(
                           R2[:, hc * SQ:(hc + 1) * SQ], R4[:, sgs:sgs + SQ], ps2),
                       reads=(sgn, pn2), writes=(f"R2s{hc}",))
                if b == 0 and sq == 0:
                    ffn_tasks.pop(5)
            if b == 0:
                ffn_tasks.flush()
            for oc in range(NT):
                yp = oc % 2
                w3o = W3O + yp * FH
                if oc + 1 < NT:
                    ffn3_load(oc + 1)
                if oc == 2 and sq + 1 < NSQ:
                    ffn12_load(1)
                fns = []
                for ht in range(NHT):
                    fns.append(lambda e, ht=ht, yp=yp, w3o=w3o: e.matmul(
                        PBs(yp), R2[:, w3o + ht * 128: w3o + (ht + 1) * 128],
                        R2[:, ht * SQ:(ht + 1) * SQ],
                        start=(ht == 0), stop=(ht == NHT - 1)))
                P.step("tensor", fns, reads=(f"R2F_w3{yp}",) + hid_names, writes=(f"pb{yp}",))
                yofs = SQ + yp * SQ
                P.step("scalar",
                       lambda e, yofs=yofs, yp=yp: e.activation(
                           R4[:, yofs:yofs + SQ], PBs(yp), AF.Copy, bias=0.0, scale=1.0),
                       reads=(f"pb{yp}",), writes=(f"R4_y{yp}",))
                P.step("sync",
                       lambda e, yofs=yofs, oc=oc, sq=sq: e.dma_start(
                           ysp_d[oc][:, sq * SQ:(sq + 1) * SQ], R4[:, yofs:yofs + SQ]),
                       reads=(f"R4_y{yp}",), writes=(f"ysp{oc}_{sq}",), sem="Q", inc=16)
                if debug:
                    P.step("gpsimd",
                           lambda e, yofs=yofs, oc=oc, sq=sq: e.dma_start(
                               yspdbg_d[oc][:, sq * SQ:(sq + 1) * SQ], R4[:, yofs:yofs + SQ]),
                           reads=(f"R4_y{yp}",), writes=(f"yspdbg{oc}_{sq}",), sem="G", inc=16)
                P.step("tensor",
                       lambda e, yofs=yofs, oc=oc: e.matmul(
                           PBs(2), ones_bf[:], R4[:, yofs:yofs + SQ],
                           start=(oc == 0), stop=(oc == NT - 1), skip_group_check=True),
                       reads=(f"R4_y{yp}", "ones_bf"), writes=("pb2",))
                P.step("scalar",
                       lambda e, yofs=yofs: e.activation(
                           R6[:, 0:512], R4[:, yofs:yofs + SQ], AF.Square,
                           bias=0.0, scale=1.0),
                       reads=(f"R4_y{yp}",), writes=("exp0",))
                P.step("tensor",
                       lambda e, oc=oc: e.matmul(
                           PBs(3), ones_bf[:], R6[:, 0:512],
                           start=(oc == 0), stop=(oc == NT - 1), skip_group_check=True),
                       reads=("exp0", "ones_bf"), writes=("pb3",))
            c0 = sq * SQ
            P.step("vector",
                   lambda e, c0=c0: e.tensor_scalar(
                       R9a[:, c0:c0 + SQ], PBs(3), 1.0 / D, None, ALU.mult),
                   reads=("pb3",), writes=("bcr",))
            P.step("vector",
                   lambda e, c0=c0: e.tensor_scalar(
                       R8b[:, c0:c0 + SQ], PBs(2), 1.0 / D, None, ALU.mult),
                   reads=("pb2",), writes=("R8b",))
            P.step("vector",
                   lambda e, c0=c0: e.tensor_mul(
                       R9b[:, 0:512], R8b[:, c0:c0 + SQ], R8b[:, c0:c0 + SQ]),
                   reads=("R8b",), writes=("R9b",))
            P.step("vector",
                   lambda e, c0=c0: e.tensor_sub(
                       R9a[:, c0:c0 + SQ], R9a[:, c0:c0 + SQ], R9b[:, 0:512]),
                   reads=("bcr", "R9b"), writes=("bcr",))
            P.step("scalar",
                   lambda e, c0=c0: e.activation(
                       R9b[:, 0:512], R9a[:, c0:c0 + SQ], AF.Ln, bias=epscol[:], scale=1.0),
                   reads=("bcr", "epscol"), writes=("R9b",))
            P.step("scalar",
                   lambda e, c0=c0: e.activation(
                       R9a[:, c0:c0 + SQ], R9b[:, 0:512], AF.Exp, bias=0.0, scale=-0.5),
                   reads=("R9b",), writes=("bcr",))
            P.step("vector",
                   lambda e, c0=c0: e.tensor_mul(
                       R9c[:, c0:c0 + SQ], R8b[:, c0:c0 + SQ], R9a[:, c0:c0 + SQ]),
                   reads=("R8b", "bcr"), writes=("bcmr",))

        if b == 0:
            ffn_tasks.sync_point()

        P.claim("R3")
        P.claim("R10")
        for t in range(NT):
            yofs = (t % 2) * S
            yn = f"R3_y{t % 2}"
            P.step("sync", lambda e, t=t, yofs=yofs: e.dma_start(
                R3[:, yofs:yofs + S], ysp_d[t]),
                reads=tuple(f"ysp{t}_{sq}" for sq in range(NSQ)),
                writes=(yn,), sem="Q", inc=16)
            P.step("sync", lambda e, t=t: e.dma_start(R8a[:], x1sp_d[t]),
                   reads=(f"x1sp{t}",), writes=("R8a",), sem="Q", inc=16)
            fw, fwn = (R10[:, 0:S], "R10") if t % 2 == 0 else (R8b[:], "R8b")
            P.step("gpsimd",
                   lambda e, fw=fw, yofs=yofs: e.tensor_mul(
                       fw[:], R3[:, yofs:yofs + S], R9a[:]),
                   reads=(yn, "bcr"), writes=(fwn,))
            P.step("vector", lambda e, fw=fw: e.tensor_sub(fw[:], fw[:], R9c[:]),
                   reads=(fwn, "bcmr"), writes=(fwn,))
            P.step("vector", lambda e, fw=fw: e.tensor_add(fw[:], fw[:], R8a[:]),
                   reads=(fwn, "R8a"), writes=(fwn,))
            P.step("sync", lambda e, fw=fw, t=t, b=b: e.dma_start(
                out_d[b][t * 128:(t + 1) * 128, :], fw[:]),
                reads=(fwn,), writes=(f"outd{b}_{t}",), sem="Q", inc=16)
        P.claim("R1_")
        P.claim("R2")
        P.claim("R4")

    assert att_tasks.h1 == len(att_tasks.tasks), "gamma tasks left"
    assert mid_tasks.h1 == len(mid_tasks.tasks), "w1/w2 quant tasks left"
    assert ffn_tasks.h1 == len(ffn_tasks.tasks), "w3 quant tasks left"

    for i in range(DMA_POOL):
        P.streams["sync"].append(("w", "Q" + str(i), P.cnt["Q" + str(i)]))
        P.streams["gpsimd"].append(("w", "G" + str(i), P.cnt["G" + str(i)]))
        P.streams["scalar"].append(("w", "C" + str(i), P.cnt["C" + str(i)]))

    with nc.Block() as block:
        def runner(eng_name):
            def go(e):
                for entry in P.streams[eng_name]:
                    if entry[0] == "w":
                        e.wait_ge(sems[entry[1]], entry[2])
                    else:
                        _, f, semname, inc = entry
                        inst = f(e)
                        if semname is not None:
                            inst.then_inc(sems[semname], inc)
            return go
        block.sync(runner("sync"))
        block.gpsimd(runner("gpsimd"))
        block.vector(runner("vector"))
        block.scalar(runner("scalar"))
        block.tensor(runner("tensor"))

    ctxmgr.close()
    return nc


def _perm_idx(H):
    idx = np.zeros(H * 128, np.int64)
    for h in range(H):
        for j in range(64):
            idx[h * 128 + j] = h * 128 + 2 * j
            idx[h * 128 + 64 + j] = h * 128 + 2 * j + 1
    return idx


def _pack(wT, NC, CC):
    Din, Dout = wT.shape
    NTl = Din // 128
    assert Dout == NC * CC
    a = wT.reshape(NTl, 128, NC, CC)
    return np.ascontiguousarray(a.transpose(2, 1, 0, 3).reshape(NC, 128, NTl * CC))


def _trig_table(S):
    j = np.arange(64, dtype=np.float64)
    theta = 10000.0 ** (-j / 64.0)
    pos = np.arange(S, dtype=np.float64)
    fr = pos[None, :] * theta[:, None]
    trig = np.zeros((128, S), np.float64)
    trig[0:64] = np.cos(fr)
    trig[64:128] = np.sin(fr)
    return trig.astype(np.float32)


def pack_core_inputs(cfg, x_core, wq, bq, wk, bk, wv, bv, wo, bo, w1, w2, w3):
    H, NT, NHT, NG, VG = cfg["H"], cfg["NT"], cfg["NHT"], cfg["NG"], cfg["VG"]
    S = cfg["S"]
    perm = _perm_idx(H)
    m = {
        "xt": x_core.transpose(0, 2, 1),
        "wq_p": _pack(np.ascontiguousarray(wq.T[:, perm]), H, 128),
        "wk_p": _pack(np.ascontiguousarray(wk.T[:, perm]), H, 128),
        "wv_p": _pack(np.ascontiguousarray(wv.T), H, 128),
        "wo_p": _pack(np.ascontiguousarray(wo.T), NT, 128),
        "w1_p": _pack(np.ascontiguousarray(w1.T), NHT, 128),
        "w2_p": _pack(np.ascontiguousarray(w2.T), NHT, 128),
        "w3_p": _pack(np.ascontiguousarray(w3.T), NT, 128),
        "bq_p": bq[perm].reshape(H, 128).T,
        "bk_p": bk[perm].reshape(H, 128).T,
        "bv_p": bv.reshape(H, 128).T,
        "bo_p": bo.reshape(NT, 128).T,
        "trig": _trig_table(S),
    }
    return {k: np.ascontiguousarray(v, dtype=np.float32) for k, v in m.items()}


def kernel(L0_attn_x, L0_attn_wq, L0_attn_bq, L0_attn_wk, L0_attn_bk,
           L0_attn_wv, L0_attn_bv, L0_attn_wo, L0_attn_bo,
           L0_ffn_w1, L0_ffn_w2, L0_ffn_w3):
    cfg = make_cfg()
    NCORES = 8
    NB = cfg["NB"]
    x = np.asarray(L0_attn_x, dtype=np.float32)
    wq, bq, wk, bk, wv, bv, wo, bo, w1, w2, w3 = [
        np.asarray(a, dtype=np.float32) for a in
        (L0_attn_wq, L0_attn_bq, L0_attn_wk, L0_attn_bk, L0_attn_wv,
         L0_attn_bv, L0_attn_wo, L0_attn_bo, L0_ffn_w1, L0_ffn_w2, L0_ffn_w3)]

    nc = build_program(cfg)
    in_maps = []
    for c in range(NCORES):
        xc = x[c * NB:(c + 1) * NB]
        in_maps.append(pack_core_inputs(cfg, xc, wq, bq, wk, bk, wv, bv,
                                        wo, bo, w1, w2, w3))
    res = run_bass_kernel_spmd(nc, in_maps, core_ids=list(range(NCORES)))
    outs = [r["out"] for r in res.results]
    full = np.concatenate(outs, axis=0)
    return np.ascontiguousarray(full.transpose(0, 2, 1))


# version 25
# speedup vs baseline: 1.0581x; 1.0259x over previous
import contextlib
import numpy as np

import concourse.bass as bass
import concourse.mybir as mybir
from concourse.bass_utils import run_bass_kernel_spmd

F32 = mybir.dt.float32
BF16 = mybir.dt.bfloat16
ALU = mybir.AluOpType
AF = mybir.ActivationFunctionType
AX = mybir.AxisListType.X

INV_SQRT_HD = 0.08838834764831845
C_SCALE = 1.02 * INV_SQRT_HD / 2.0
LN_EPS = 1e-5
Q_EPS = 1e-8


def make_cfg(D=2048, S=2048, H=16, FH=5632, NB=2):
    c = dict(D=D, S=S, H=H, FH=FH, NB=NB)
    c["NT"] = D // 128
    c["NHT"] = FH // 128
    c["NKT"] = S // 128
    c["NQC"] = S // 512
    c["VG"] = 2
    c["NG"] = H // 2
    c["SQ"] = 512
    c["NSQ"] = S // 512
    return c


ENGSEM = {"sync": "Q", "gpsimd": "P", "vector": "V", "scalar": "S", "tensor": "T"}


DMA_POOL = 8


class Prog:

    def __init__(self):
        self.streams = {e: [] for e in ("sync", "gpsimd", "vector", "scalar", "tensor")}
        self.cnt = {}
        self.semnames = []
        for k in "VSTP":
            self.cnt[k] = 0
            self.semnames.append(k)
        for q in "QGC":
            for i in range(DMA_POOL):
                self.cnt[q + str(i)] = 0
                self.semnames.append(q + str(i))
        self.qrot = {q: 0 for q in "QGC"}
        self.waited = {e: {} for e in self.streams}
        self.bufs = {}
        self.pending = {}

    def _buf(self, name):
        return self.bufs.setdefault(name, {"w": None, "r": {}})

    def wait(self, eng, tok):
        sem, val = tok
        if val <= 0:
            return
        w = self.waited[eng]
        if w.get(sem, 0) >= val:
            return
        w[sem] = val
        self.streams[eng].append(("w", sem, val))

    def step(self, eng, fns, reads=(), writes=(), sem=None, inc=1):
        if callable(fns):
            fns = [fns]
        for b in reads:
            st = self._buf(b)
            if st["w"]:
                self.wait(eng, st["w"])
        for b in writes:
            for pref, toks in self.pending.items():
                if b.startswith(pref):
                    for tok in toks:
                        self.wait(eng, tok)
            st = self._buf(b)
            if st["w"]:
                self.wait(eng, st["w"])
            for s, v in st["r"].items():
                self.wait(eng, (s, v))
        if sem is None:
            sem = ENGSEM[eng]
        if sem in ("Q", "G", "C"):
            i = self.qrot[sem]
            self.qrot[sem] = (i + 1) % DMA_POOL
            sem = sem + str(i)
            self.wait(eng, (sem, self.cnt[sem]))
        self.cnt[sem] += inc
        tok = (sem, self.cnt[sem])
        for f in fns[:-1]:
            self.streams[eng].append(("i", f, None, 0))
        self.streams[eng].append(("i", fns[-1], sem, inc))
        for b in reads:
            st = self._buf(b)
            st["r"][tok[0]] = max(st["r"].get(tok[0], 0), tok[1])
        for b in writes:
            self.bufs[b] = {"w": tok, "r": {}}
        return tok

    def claim(self, prefix):
        toks = {}
        for name in [n for n in self.bufs if n.startswith(prefix)]:
            st = self.bufs.pop(name)
            if st["w"]:
                s, v = st["w"]
                toks[s] = max(toks.get(s, 0), v)
            for s, v in st["r"].items():
                toks[s] = max(toks.get(s, 0), v)
        self.pending[prefix] = list(toks.items())


def build_program(cfg, debug=False):
    D, S, H, FH, NB = cfg["D"], cfg["S"], cfg["H"], cfg["FH"], cfg["NB"]
    NT, NHT, NKT, NQC = cfg["NT"], cfg["NHT"], cfg["NKT"], cfg["NQC"]
    VG, NG, SQ, NSQ = cfg["VG"], cfg["NG"], cfg["SQ"], cfg["NSQ"]
    PCW = min(2048, S)
    W3C = (FH + PCW - 1) // PCW
    assert S == D, "R3 role layout assumes S == D"

    nc = bass.Bass("TRN2", target_bir_lowering=False)

    xt_d = nc.dram_tensor("xt", [NB, D, S], F32, kind="ExternalInput").ap()
    wq_d = nc.dram_tensor("wq_p", [H, 128, D], F32, kind="ExternalInput").ap()
    wk_d = nc.dram_tensor("wk_p", [H, 128, D], F32, kind="ExternalInput").ap()
    wv_d = nc.dram_tensor("wv_p", [H, 128, D], F32, kind="ExternalInput").ap()
    wo_d = nc.dram_tensor("wo_p", [NT, 128, D], F32, kind="ExternalInput").ap()
    w1_d = nc.dram_tensor("w1_p", [NHT, 128, D], F32, kind="ExternalInput").ap()
    w2_d = nc.dram_tensor("w2_p", [NHT, 128, D], F32, kind="ExternalInput").ap()
    w3_d = nc.dram_tensor("w3_p", [NT, 128, FH], F32, kind="ExternalInput").ap()
    bq_d = nc.dram_tensor("bq_p", [128, H], F32, kind="ExternalInput").ap()
    bk_d = nc.dram_tensor("bk_p", [128, H], F32, kind="ExternalInput").ap()
    bv_d = nc.dram_tensor("bv_p", [128, H], F32, kind="ExternalInput").ap()
    bo_d = nc.dram_tensor("bo_p", [128, NT], F32, kind="ExternalInput").ap()
    trig_d = nc.dram_tensor("trig", [128, S], F32, kind="ExternalInput").ap()
    out_d = nc.dram_tensor("out", [NB, D, S], F32, kind="ExternalOutput").ap()
    w1q_d = nc.dram_tensor("w1q", [NHT, 128, D], BF16, kind="Internal").ap()
    w2q_d = nc.dram_tensor("w2q", [NHT, 128, D], BF16, kind="Internal").ap()
    w3q_d = nc.dram_tensor("w3q", [NT, 128, FH], BF16, kind="Internal").ap()
    x1sp_d = nc.dram_tensor("x1sp", [NT, 128, S], F32,
                            kind="ExternalOutput" if debug else "Internal").ap()
    osp_d = nc.dram_tensor("osp", [NT, 128, S], BF16, kind="Internal").ap()
    ysp_d = nc.dram_tensor("ysp", [NT, 128, S], BF16, kind="Internal").ap()
    if debug:
        ospdbg_d = nc.dram_tensor("ospdbg", [NT, 128, S], F32, kind="ExternalOutput").ap()
        yspdbg_d = nc.dram_tensor("yspdbg", [NT, 128, S], F32, kind="ExternalOutput").ap()
        w1qdbg_d = nc.dram_tensor("w1qdbg", [NHT, 128, D], F32, kind="ExternalOutput").ap()
        w2qdbg_d = nc.dram_tensor("w2qdbg", [NHT, 128, D], F32, kind="ExternalOutput").ap()
        w3qdbg_d = nc.dram_tensor("w3qdbg", [NT, 128, FH], F32, kind="ExternalOutput").ap()
        dbg_q = {"w1": w1qdbg_d, "w2": w2qdbg_d, "w3": w3qdbg_d}

    cur = [((nc.sbuf_base + 31) // 32 * 32)]
    limit = nc.SBUF_PARTITION_SIZE_BYTES - 32

    def carve(name, shape, dtype):
        h = nc.alloc_sbuf_tensor_at(name, shape, dtype, offset=cur[0])
        per = 1
        for s in shape[1:]:
            per *= s
        per *= mybir.dt.size(dtype)
        cur[0] += (per + 31) // 32 * 32
        assert cur[0] <= limit, f"SBUF overflow at {name}: {cur[0]} > {limit}"
        return h

    R1 = carve("R1", [128, NT * S], BF16)
    R2 = carve("R2", [128, max(NT * S, NHT * SQ + 2 * FH)], BF16)
    R3 = carve("R3", [128, 4 * D], BF16)
    R4 = carve("R4", [128, max(2 * S, 5 * SQ)], BF16)
    R5 = carve("R5", [128, S], BF16)
    S_R6 = max(2048, S)
    R6 = carve("R6", [128, S_R6], BF16)
    R8a = carve("R8a", [128, S], F32)
    R8b = carve("R8b", [128, S], F32)
    R9a = carve("R9a", [128, S], F32)
    R9b = carve("R9b", [128, 512], F32)
    R9c = carve("R9c", [128, S], BF16)
    R10 = carve("R10", [128, max(2048, S)], F32)
    trig_sb = carve("trig_sb", [128, S], BF16)
    ones_bf = carve("ones_bf", [128, 128], BF16)
    onesrow = carve("onesrow", [1, 128], F32)
    onescol = carve("onescol", [128, 1], F32)
    epscol = carve("epscol", [128, 1], F32)
    bq_sb = carve("bq_sb", [128, H], F32)
    bk_sb = carve("bk_sb", [128, H], F32)
    bv_sb = carve("bv_sb", [128, H], F32)
    bo_sb = carve("bo_sb", [128, NT], F32)
    gpart = carve("gpart", [128, 64], F32)
    smallc = carve("smallc", [128, 8], F32)
    smallr = carve("smallr", [1, 16], F32)

    P = Prog()
    P_SEMNAMES = P.semnames
    ctxmgr = contextlib.ExitStack()
    PA = ctxmgr.enter_context(nc.psum_tensor("PA", [128, 2048], F32))
    PB = ctxmgr.enter_context(nc.psum_tensor("PB", [128, 2048], F32))
    sems = {k: ctxmgr.enter_context(nc.semaphore("sem" + k)) for k in P_SEMNAMES}

    def PAs(i):
        return PA[:, i * 512:(i + 1) * 512]

    def PBs(i):
        return PB[:, i * 512:(i + 1) * 512]

    P.step("vector", lambda e: e.memset(ones_bf[:], 1.0), writes=("ones_bf",))
    P.step("vector", lambda e: e.memset(onesrow[:], 1.0), writes=("onesrow",))
    P.step("vector", lambda e: e.memset(onescol[:], 1.0), writes=("onescol",))
    P.step("vector", lambda e: e.memset(epscol[:], LN_EPS), writes=("epscol",))
    P.step("gpsimd", lambda e: e.dma_start(trig_sb[:], trig_d[:]), writes=("trig",), sem="G", inc=16)
    P.step("sync", lambda e: e.dma_start(bq_sb[:], bq_d[:]), writes=("bq",), sem="Q", inc=16)
    P.step("sync", lambda e: e.dma_start(bk_sb[:], bk_d[:]), writes=("bk",), sem="Q", inc=16)
    P.step("sync", lambda e: e.dma_start(bv_sb[:], bv_d[:]), writes=("bv",), sem="Q", inc=16)
    P.step("sync", lambda e: e.dma_start(bo_sb[:], bo_d[:]), writes=("bo",), sem="Q", inc=16)

    t1l = [(w1_d[i], D, w1q_d[i], f"w1q{i}") for i in range(NHT)]
    t2l = [(w2_d[i], D, w2q_d[i], f"w2q{i}") for i in range(NHT)]
    t3l = []
    for oc in range(NT):
        for j in range(W3C):
            c0, c1 = j * PCW, min((j + 1) * PCW, FH)
            t3l.append((w3_d[oc][:, c0:c1], c1 - c0, w3q_d[oc][:, c0:c1], f"w3q{oc}_{j}"))
    prep_mats = [("w1", t1l, NHT * 128 * D), ("w2", t2l, NHT * 128 * D),
                 ("w3", t3l, NT * 128 * FH)]
    scratch_reads = {}

    def quarters(ncols):
        qs = []
        c = 0
        while c < ncols:
            qs.append((c, min(ncols, c + 512)))
            c += 512
        return qs

    def gamma_quarter_tasks(tile_idx, gq0, tile):
        src, ncols, _, _ = tile
        qs = quarters(ncols)
        tasks = []
        nq = len(qs)
        for qi, (c0, c1) in enumerate(qs):
            slot = 1 + (gq0 + qi) % 3
            stg = R10[:, slot * 512: slot * 512 + (c1 - c0)]
            stn = f"R10q{slot}"

            def p0(src=src, c0=c0, c1=c1, stg=stg, stn=stn):
                P.step("sync", lambda e: e.dma_start(stg, src[:, c0:c1]),
                       writes=(stn,), sem="Q", inc=16)

            def p1(qi=qi, stg=stg, stn=stn, tile_idx=tile_idx,
                   last=(qi == len(qs) - 1), nq=nq):
                P.step("vector",
                       lambda e: e.tensor_reduce(gpart[:, 56 + qi % 4: 57 + qi % 4],
                                                 stg, AX, ALU.add,
                                                 apply_absolute_value=True),
                       reads=(stn,), writes=("gpartq",))
                if last:
                    P.step("vector",
                           lambda e: e.tensor_reduce(
                               gpart[:, tile_idx:tile_idx + 1],
                               gpart[:, 56:56 + nq], AX, ALU.add),
                           reads=("gpartq",), writes=("gpart",))
            tasks.append((p0, p1))
        return tasks

    def gamma_fin_task(mi, ntiles, nelem):
        c4, c5, c6 = 4 + 3 * mi, 5 + 3 * mi, 6 + 3 * mi
        gc = 60 + mi

        def pA():
            P.step("vector",
                   lambda e: e.tensor_reduce(gpart[:, gc:gc + 1], gpart[:, 0:ntiles], AX, ALU.add),
                   reads=("gpart",), writes=(f"gfin{mi}",))

        def p1():
            P.step("tensor",
                   lambda e: e.matmul(PBs(0)[0:1, 0:1], onescol[:], gpart[:, gc:gc + 1],
                                      start=True, stop=True),
                   reads=(f"gfin{mi}", "onescol"), writes=("pb0",))
            P.step("scalar", lambda e: e.copy(smallr[0:1, c4:c4 + 1], PBs(0)[0:1, 0:1]),
                   reads=("pb0",), writes=(f"gsum{mi}",))
            P.step("vector",
                   lambda e: e.tensor_scalar(smallr[0:1, c5:c5 + 1], smallr[0:1, c4:c4 + 1],
                                             1.0 / nelem, Q_EPS, ALU.mult, ALU.add),
                   reads=(f"gsum{mi}",), writes=(f"gden{mi}",))
            P.step("vector", lambda e: e.reciprocal(smallr[0:1, c6:c6 + 1], smallr[0:1, c5:c5 + 1]),
                   reads=(f"gden{mi}",), writes=(f"grec{mi}",))
            P.step("tensor",
                   lambda e: e.matmul(PBs(0)[:, 0:1], onesrow[:], smallr[0:1, c6:c6 + 1],
                                      start=True, stop=True),
                   reads=(f"grec{mi}", "onesrow"), writes=("pb0",))
            P.step("scalar", lambda e: e.copy(smallc[:, 1 + mi:2 + mi], PBs(0)[:, 0:1]),
                   reads=("pb0",), writes=(f"rgcol{mi}",))
        return ((lambda: None, pA), (lambda: None, p1))

    def quant_quarter_tasks(mi, qidx0, tile, dbg_dst=None):
        src, ncols, dst, bufname = tile
        qs = quarters(ncols)
        names = []
        tasks = []
        for qi, (c0, c1) in enumerate(qs):
            slot = (qidx0 + qi) % 4
            w = c1 - c0
            stg = R10[:, slot * 512: slot * 512 + w]
            stn = f"R10q{slot}"
            qn = f"{bufname}q{qi}"
            names.append(qn)

            def p0(src=src, c0=c0, c1=c1, stg=stg, stn=stn):
                P.step("sync", lambda e: e.dma_start(stg, src[:, c0:c1]),
                       writes=(stn,), sem="Q", inc=16)

            def p1(c0=c0, c1=c1, w=w, stg=stg, stn=stn, qn=qn, dbg_dst=dbg_dst):
                P.step("vector",
                       lambda e: e.tensor_scalar(
                           R9b[:, 0:w], stg, smallc[:, 1 + mi:2 + mi], -0.5,
                           ALU.mult, ALU.is_le),
                       reads=(stn, f"rgcol{mi}"), writes=("R9b",))
                P.step("vector",
                       lambda e: e.tensor_scalar(
                           stg, stg, smallc[:, 1 + mi:2 + mi], 0.5,
                           ALU.mult, ALU.is_ge),
                       reads=(stn, f"rgcol{mi}"), writes=(stn,))
                P.step("vector",
                       lambda e: e.tensor_sub(stg, stg, R9b[:, 0:w]),
                       reads=(stn, "R9b"), writes=(stn,))
                P.step("gpsimd", lambda e: e.dma_start(dst[:, c0:c1], stg),
                       reads=(stn,), writes=(qn,), sem="G", inc=16)
                if dbg_dst is not None:
                    P.step("sync", lambda e: e.dma_start(dbg_dst[:, c0:c1], stg),
                           reads=(stn,), writes=(qn + "dbg",), sem="Q", inc=16)
            tasks.append((p0, p1))
        scratch_reads.setdefault(bufname, tuple(names))
        return tasks

    class SplitTasks:
        def __init__(self, tasks, look=3):
            self.tasks = tasks
            self.look = look
            self.h0 = 0
            self.h1 = 0

        def pop(self, n):
            for _ in range(n):
                while self.h0 < len(self.tasks) and self.h0 < self.h1 + self.look:
                    self.tasks[self.h0][0]()
                    self.h0 += 1
                if self.h1 < len(self.tasks):
                    self.tasks[self.h1][1]()
                    self.h1 += 1

        def flush(self):
            self.pop(len(self.tasks))

        def sync_point(self):
            while self.h1 < self.h0:
                self.tasks[self.h1][1]()
                self.h1 += 1

    gl = []
    pendB = []

    def push(t):
        gl.append(t)
        for p in pendB:
            p[0] -= 1
        while pendB and pendB[0][0] <= 0:
            gl.append(pendB.pop(0)[1])

    quants = {}
    qct = 0
    gqc = 0
    for mi, (nm, tiles, nelem) in enumerate(prep_mats):
        for i, tile in enumerate(tiles):
            for t in gamma_quarter_tasks(i, gqc, tile):
                push(t)
            gqc += len(quarters(tile[1]))
        fA, fB = gamma_fin_task(mi, len(tiles), nelem)
        gl.append(fA)
        pendB.append([12, fB])
        qs = []
        for ti, tile in enumerate(tiles):
            dd = None
            if debug:
                src, ncols, dst, bufname = tile
                if nm in ("w1", "w2"):
                    dd = dbg_q[nm][ti]
                else:
                    oc, j = ti // W3C, ti % W3C
                    dd = dbg_q["w3"][oc][:, j * PCW:j * PCW + ncols]
            qs.extend(quant_quarter_tasks(mi, qct, tile, dd))
            qct += len(quarters(tile[1]))
        quants[nm] = qs
    while pendB:
        gl.append(pendB.pop(0)[1])
    att_tasks = SplitTasks(gl, look=2)
    mid_tasks = SplitTasks(quants.get("w1", []) + quants.get("w2", []), look=3)
    ffn_tasks = SplitTasks(quants.get("w3", []), look=3)

    def ln_stats_pipeline(n_tiles, srcgen):
        exp_all = tuple(f"exp{i}" for i in range(4))
        pend = None
        for t in range(n_tiles):
            src_ap, src_reads, is_f32 = srcgen(t)
            if t % 2 == 0:
                cb, cbn, sqb, sqn = R5[:, 0:S], ("R5full",), R6[:, 0:S], exp_all
            else:
                cb, cbn, sqb, sqn = R4[:, 0:S], ("R4a",), R4[:, S:2 * S], ("R4b",)
            if is_f32:
                P.step("vector", lambda e, cb=cb, src_ap=src_ap: e.tensor_copy(cb, src_ap),
                       reads=tuple(src_reads), writes=cbn)
                sum_ap, sum_reads = cb, cbn
            else:
                sum_ap, sum_reads = src_ap, tuple(src_reads)
            P.step("scalar", lambda e, sqb=sqb, src_ap=src_ap: e.activation(
                sqb, src_ap, AF.Square, bias=0.0, scale=1.0),
                reads=tuple(src_reads), writes=sqn)
            if pend is not None:
                pend()

            def mk(t=t, sum_ap=sum_ap, sum_reads=sum_reads, sqb=sqb, sqn=sqn):
                def go():
                    for qc in range(NQC):
                        P.step("tensor",
                               lambda e, qc=qc: e.matmul(
                                   PAs(qc), ones_bf[:], sum_ap[:, qc * 512:(qc + 1) * 512],
                                   start=(t == 0), stop=(t == n_tiles - 1),
                                   skip_group_check=True),
                               reads=("ones_bf",) + tuple(sum_reads), writes=(f"pa{qc}",))
                        P.step("tensor",
                               lambda e, qc=qc: e.matmul(
                                   PBs(qc), ones_bf[:], sqb[:, qc * 512:(qc + 1) * 512],
                                   start=(t == 0), stop=(t == n_tiles - 1),
                                   skip_group_check=True),
                               reads=("ones_bf",) + tuple(sqn), writes=(f"pb{qc}",))
                return go
            pend = mk()
        pend()

    def ln_rows():
        for qc in range(NQC):
            c0, c1 = qc * 512, (qc + 1) * 512
            P.step("vector",
                   lambda e, c0=c0, c1=c1, qc=qc: e.tensor_scalar(
                       R9a[:, c0:c1], PBs(qc), 1.0 / D, None, ALU.mult),
                   reads=(f"pb{qc}",), writes=("bcr",))
            P.step("vector",
                   lambda e, c0=c0, c1=c1, qc=qc: e.tensor_scalar(
                       R8b[:, c0:c1], PAs(qc), 1.0 / D, None, ALU.mult),
                   reads=(f"pa{qc}",), writes=("R8b",))
            P.step("vector",
                   lambda e, c0=c0, c1=c1: e.tensor_mul(
                       R9b[:, 0:512], R8b[:, c0:c1], R8b[:, c0:c1]),
                   reads=("R8b",), writes=("R9b",))
            P.step("vector",
                   lambda e, c0=c0, c1=c1: e.tensor_sub(
                       R9a[:, c0:c1], R9a[:, c0:c1], R9b[:, 0:512]),
                   reads=("bcr", "R9b"), writes=("bcr",))
            P.step("scalar",
                   lambda e, c0=c0, c1=c1: e.activation(
                       R9b[:, 0:512], R9a[:, c0:c1], AF.Ln, bias=epscol[:], scale=1.0),
                   reads=("bcr", "epscol"), writes=("R9b",))
            P.step("scalar",
                   lambda e, c0=c0, c1=c1: e.activation(
                       R9a[:, c0:c1], R9b[:, 0:512], AF.Exp, bias=0.0, scale=-0.5),
                   reads=("R9b",), writes=("bcr",))
            P.step("vector",
                   lambda e, c0=c0, c1=c1: e.tensor_mul(
                       R9c[:, c0:c1], R8b[:, c0:c1], R9a[:, c0:c1]),
                   reads=("R8b", "bcr"), writes=("bcmr",))

    xn_names = tuple(f"R1_t{t}" for t in range(NT))
    ctx_names = tuple(f"R2_h{h}" for h in range(H))
    hid_names = tuple(f"R2s{hc}" for hc in range(NHT))
    vtok_names = tuple(f"vtok{tt}" for tt in range(NKT))

    for b in range(NB):
        P.claim("R4")
        P.claim("R5")

        def l1_src(t):
            x, xn = (R8a, "R8a") if t % 2 == 0 else (R8b, "R8b")
            P.step("sync", lambda e, t=t, x=x, b=b: e.dma_start(
                x[:], xt_d[b][t * 128:(t + 1) * 128, :]),
                writes=(xn,), sem="Q", inc=16)
            return x[:], (xn,), True
        ln_stats_pipeline(NT, l1_src)
        ln_rows()
        for t in range(NT):
            x, xn = (R8a, "R8a") if t % 2 == 0 else (R8b, "R8b")
            P.step("sync", lambda e, t=t, x=x, b=b: e.dma_start(
                x[:], xt_d[b][t * 128:(t + 1) * 128, :]),
                writes=(xn,), sem="Q", inc=16)
            sl = R1[:, t * S:(t + 1) * S]
            P.step("vector", lambda e, sl=sl, x=x: e.tensor_mul(sl, x[:], R9a[:]),
                   reads=(xn, "bcr"), writes=(f"R1_t{t}",))
            P.step("vector", lambda e, sl=sl: e.tensor_sub(sl, sl, R9c[:]),
                   reads=(f"R1_t{t}", "bcmr"), writes=(f"R1_t{t}",))

        P.claim("R3")
        P.claim("R4")
        P.claim("R5")
        def emit_wload(h):
            P.step("gpsimd", lambda e, h=h: e.dma_start(R3[:, 0:D], wq_d[h]),
                   writes=("R3_wq",), sem="G", inc=16)
            P.step("gpsimd", lambda e, h=h: e.dma_start(R3[:, D:2 * D], wk_d[h]),
                   writes=("R3_wk",), sem="G", inc=16)
            P.step("gpsimd", lambda e, h=h: e.dma_start(R3[:, 2 * D:3 * D], wv_d[h]),
                   writes=("R3_wv",), sem="G", inc=16)

        emit_wload(0)
        prev_chain = None
        chain_defer_qc = 1 if ((NQC - 1) % 2 == 1 and NQC >= 4) else None
        for h in range(H):
            if chain_defer_qc is None and prev_chain is not None:
                prev_chain()
                prev_chain = None
                if b == 0:
                    att_tasks.pop(5)
            qk_group_i = 0
            for wbufn, wofs, PS, PSfull, pnm, qkT, qkn, bsb, bn in (
                    ("R3_wq", 0, PAs, PA, "pa", R4[:, 0:S], "R4_q", bq_sb, "bq"),
                    ("R3_wk", D, PBs, PB, "pb", R4[:, S:2 * S], "R4_k", bk_sb, "bk")):
                for qc in range(NQC):
                    fns = []
                    for d in range(NT):
                        fns.append(lambda e, d=d, qc=qc, PS=PS, wofs=wofs: e.matmul(
                            PS(qc),
                            R3[:, wofs + d * 128: wofs + (d + 1) * 128],
                            R1[:, d * S + qc * 512: d * S + (qc + 1) * 512],
                            start=(d == 0), stop=(d == NT - 1)))
                    P.step("tensor", fns, reads=(wbufn,) + xn_names,
                           writes=(f"{pnm}{qc}",))
                    if qk_group_i == chain_defer_qc and prev_chain is not None:
                        prev_chain()
                        prev_chain = None
                        if b == 0:
                            att_tasks.pop(5)
                    qk_group_i += 1
                P.step("scalar",
                       lambda e, qkT=qkT, PSfull=PSfull, bsb=bsb, h=h: e.activation(
                           qkT, PSfull[:, 0:NQC * 512], AF.Identity,
                           bias=bsb[:, h:h + 1], scale=1.0),
                       reads=tuple(f"{pnm}{qc}" for qc in range(NQC)) + (bn,),
                       writes=(qkn,))
                rsc = R2[0:64, h * S:(h + 1) * S]
                P.step("vector",
                       lambda e, qkT=qkT, rsc=rsc: e.tensor_mul(
                           rsc, qkT[64:128, :], trig_sb[64:128, :]),
                       reads=(qkn, "trig"), writes=(f"R2_h{h}",))
                P.step("vector",
                       lambda e, qkT=qkT: e.tensor_mul(
                           qkT[0:64, :], qkT[0:64, :], trig_sb[0:64, :]),
                       reads=(qkn, "trig"), writes=(qkn,))
                P.step("vector",
                       lambda e, qkT=qkT, rsc=rsc: e.tensor_sub(
                           qkT[0:64, :], qkT[0:64, :], rsc),
                       reads=(qkn, f"R2_h{h}"), writes=(qkn,))

            for qkT, qkn, PS, PSfull, pnm, ridx in (
                    (R4[:, 0:S], "R4_q", PAs, PA, "pa", 0),
                    (R4[:, S:2 * S], "R4_k", PBs, PB, "pb", 1)):
                for qc in range(NQC):
                    slot = qc % 2
                    P.step("scalar",
                           lambda e, qc=qc, slot=slot, qkT=qkT: e.activation(
                               R6[:, slot * 512:(slot + 1) * 512],
                               qkT[:, qc * 512:(qc + 1) * 512], AF.Square,
                               bias=0.0, scale=1.0),
                           reads=(qkn,), writes=(f"exp{slot}",))
                    P.step("tensor",
                           lambda e, qc=qc, slot=slot, PS=PS: e.matmul(
                               PS(qc)[0:1, :], ones_bf[:, 0:1],
                               R6[:, slot * 512:(slot + 1) * 512],
                               start=True, stop=True),
                           reads=(f"exp{slot}", "ones_bf"), writes=(f"{pnm}{qc}",))
                P.step("vector",
                       lambda e, PSfull=PSfull, ridx=ridx: e.reduce_max(
                           smallr[0:1, ridx:ridx + 1], PSfull[0:1, 0:NQC * 512], AX),
                       reads=tuple(f"{pnm}{qc}" for qc in range(NQC)),
                       writes=(f"mx{ridx}",))
            P.step("vector",
                   lambda e: e.tensor_add(smallr[0:1, 2:3], smallr[0:1, 0:1],
                                          smallr[0:1, 1:2]),
                   reads=("mx0", "mx1"), writes=("c0",))
            P.step("vector",
                   lambda e: e.tensor_scalar(smallr[0:1, 3:4], smallr[0:1, 2:3],
                                             -C_SCALE, -1.0, ALU.mult, ALU.add),
                   reads=("c0",), writes=("c1",))

            for tt in range(NKT):
                ps = PAs(3) if tt % 2 == 0 else PBs(3)
                pn = "pa3" if tt % 2 == 0 else "pb3"
                fns = []
                for d in range(NT):
                    fns.append(lambda e, d=d, tt=tt, ps=ps: e.matmul(
                        ps[:, 0:128],
                        R1[:, d * S + tt * 128: d * S + tt * 128 + 128],
                        R3[:, 2 * D + d * 128: 2 * D + (d + 1) * 128],
                        start=(d == 0), stop=(d == NT - 1)))
                P.step("tensor", fns, reads=("R3_wv",) + xn_names, writes=(pn,))
                P.step("scalar",
                       lambda e, tt=tt, ps=ps: e.activation(
                           R5[:, tt * 128:(tt + 1) * 128], ps[:, 0:128],
                           AF.Copy, bias=0.0, scale=1.0),
                       reads=(pn,), writes=(f"R5v{tt}",))
            if h + 1 < H:
                emit_wload(h + 1)

            P.step("tensor",
                   lambda e: e.matmul(PAs(0)[:, 0:1], onesrow[:], smallr[0:1, 3:4],
                                      start=True, stop=True),
                   reads=("c1", "onesrow"), writes=("pa0",))
            P.step("scalar", lambda e: e.copy(smallc[:, 0:1], PAs(0)[:, 0:1]),
                   reads=("pa0",), writes=("cbias",))

            def make_chain(qc, cpn, cps, h=h):
                sl = R2[:, h * S + qc * 512: h * S + (qc + 1) * 512]

                def chain():
                    P.step("tensor",
                           lambda e: e.matmul(PAs(2), onesrow[:], R10[0:1, 0:512],
                                              start=True, stop=True),
                           reads=("R10", "onesrow"), writes=("pa2",))
                    P.step("scalar", lambda e: e.copy(R10[:, 0:512], PAs(2)),
                           reads=("pa2",), writes=("R10",))
                    P.step("vector", lambda e, cps=cps: e.tensor_mul(sl, cps, R10[:, 0:512]),
                           reads=(cpn, "R10"), writes=(f"R2_h{h}",))
                    P.step("vector",
                           lambda e: e.tensor_scalar(
                               sl, sl, bv_sb[:, h:h + 1], None, ALU.add),
                           reads=(f"R2_h{h}", "bv"), writes=(f"R2_h{h}",))
                return chain

            LOOKAHEAD = 3
            for qc in range(NQC):
                cpn = "pa0" if qc % 2 == 0 else "pa3"
                cps = PAs(0) if qc % 2 == 0 else PAs(3)
                dps = PAs(1) if qc % 2 == 0 else PBs(3)
                dpn = "pa1" if qc % 2 == 0 else "pb3"
                for step in range(NKT + LOOKAHEAD):
                    if step < NKT:
                        kt, pb, slot = step, step % 3, step % 4
                        P.step("tensor",
                               lambda e, kt=kt, qc=qc, pb=pb: e.matmul(
                                   PBs(pb), R4[:, S + kt * 128: S + (kt + 1) * 128],
                                   R4[:, qc * 512:(qc + 1) * 512],
                                   start=True, stop=True),
                               reads=("R4_q", "R4_k"), writes=(f"pb{pb}",))
                        P.step("scalar",
                               lambda e, pb=pb, slot=slot: e.activation(
                                   R6[:, slot * 512:(slot + 1) * 512], PBs(pb), AF.Exp,
                                   bias=smallc[:, 0:1], scale=INV_SQRT_HD),
                               reads=(f"pb{pb}", "cbias"), writes=(f"exp{slot}",))
                    if step >= LOOKAHEAD:
                        kt = step - LOOKAHEAD
                        slot = kt % 4
                        P.step("tensor",
                               lambda e, kt=kt, slot=slot, dps=dps: e.matmul(
                                   dps[0:1, :], ones_bf[:, 0:1],
                                   R6[:, slot * 512:(slot + 1) * 512],
                                   start=(kt == 0), stop=(kt == NKT - 1),
                                   skip_group_check=True),
                               reads=(f"exp{slot}", "ones_bf"), writes=(dpn,))
                        P.step("tensor",
                               lambda e, kt=kt, slot=slot, cps=cps: e.matmul(
                                   cps, R5[:, kt * 128:(kt + 1) * 128],
                                   R6[:, slot * 512:(slot + 1) * 512],
                                   start=(kt == 0), stop=(kt == NKT - 1),
                                   skip_group_check=True),
                               reads=(f"exp{slot}", f"R5v{kt}"), writes=(cpn,))
                        if kt == 1 and prev_chain is not None:
                            prev_chain()
                            prev_chain = None
                            if b == 0:
                                att_tasks.pop(5)
                P.step("vector", lambda e, dps=dps: e.reciprocal(R10[0:1, 0:512], dps[0:1, :]),
                       reads=(dpn,), writes=("R10",))
                prev_chain = make_chain(qc, cpn, cps)
        prev_chain()
        prev_chain = None

        if b == 0:
            att_tasks.flush()

        P.claim("R3")
        P.claim("R10")
        def wo_load(oc):
            ws = oc % 2
            wofs = ws * D
            P.step("gpsimd", lambda e, oc=oc, wofs=wofs: e.dma_start(
                R3[:, wofs:wofs + D], wo_d[oc]),
                writes=(f"R3_wo{ws}",), sem="G", inc=16)

        wo_load(0)
        for oc in range(NT):
            ws = oc % 2
            wofs = ws * D
            if oc + 1 < NT:
                wo_load(oc + 1)
            PS, PSfull, pnm = (PAs, PA, "pa") if oc % 2 == 0 else (PBs, PB, "pb")
            for qc in range(NQC):
                fns = []
                for d in range(NT):
                    fns.append(lambda e, d=d, qc=qc, PS=PS, wofs=wofs: e.matmul(
                        PS(qc), R3[:, wofs + d * 128: wofs + (d + 1) * 128],
                        R2[:, d * S + qc * 512: d * S + (qc + 1) * 512],
                        start=(d == 0), stop=(d == NT - 1)))
                P.step("tensor", fns, reads=(f"R3_wo{ws}",) + ctx_names,
                       writes=(f"{pnm}{qc}",))
            oofs = 2 * D + ws * D
            P.step("scalar",
                   lambda e, oofs=oofs, PSfull=PSfull, oc=oc: e.activation(
                       R3[:, oofs:oofs + S], PSfull[:, 0:NQC * 512],
                       AF.Identity, bias=bo_sb[:, oc:oc + 1], scale=1.0),
                   reads=tuple(f"{pnm}{qc}" for qc in range(NQC)) + ("bo",),
                   writes=(f"R3_os{ws}",))
            P.step("sync", lambda e, oofs=oofs, oc=oc: e.dma_start(
                osp_d[oc], R3[:, oofs:oofs + S]),
                reads=(f"R3_os{ws}",), writes=(f"osp{oc}",), sem="Q", inc=16)
            if b == 0:
                mid_tasks.pop(12)
            if debug:
                P.step("gpsimd", lambda e, oofs=oofs, oc=oc: e.dma_start(
                    ospdbg_d[oc], R3[:, oofs:oofs + S]),
                    reads=(f"R3_os{ws}",), writes=(f"ospdbg{oc}",), sem="G", inc=16)

        if b == 0:
            mid_tasks.sync_point()

        P.claim("R4")
        P.claim("R5")

        def ln2_src(t):
            oofs = 2 * D + (t % 2) * D
            osn = f"R3_os{t % 2}"
            P.step("sync", lambda e, t=t, oofs=oofs: e.dma_start(
                R3[:, oofs:oofs + S], osp_d[t]),
                reads=(f"osp{t}",), writes=(osn,), sem="Q", inc=16)
            return R3[:, oofs:oofs + S], (osn,), False
        ln_stats_pipeline(NT, ln2_src)
        ln_rows()

        P.claim("R10")

        def res1_src(t):
            oofs = 2 * D + (t % 2) * D
            osn = f"R3_os{t % 2}"
            P.step("sync", lambda e, t=t, oofs=oofs: e.dma_start(
                R3[:, oofs:oofs + S], osp_d[t]),
                reads=(f"osp{t}",), writes=(osn,), sem="Q", inc=16)
            P.step("sync", lambda e, t=t, b=b: e.dma_start(
                R8a[:], xt_d[b][t * 128:(t + 1) * 128, :]),
                writes=("R8a",), sem="Q", inc=16)
            fw, fwn = (R10[:, 0:S], "R10") if t % 2 == 0 else (R8b[:], "R8b")
            P.step("gpsimd",
                   lambda e, fw=fw, oofs=oofs: e.tensor_mul(
                       fw[:], R3[:, oofs:oofs + S], R9a[:]),
                   reads=(osn, "bcr"), writes=(fwn,))
            P.step("vector", lambda e, fw=fw: e.tensor_sub(fw[:], fw[:], R9c[:]),
                   reads=(fwn, "bcmr"), writes=(fwn,))
            P.step("vector", lambda e, fw=fw: e.tensor_add(fw[:], fw[:], R8a[:]),
                   reads=(fwn, "R8a"), writes=(fwn,))
            P.step("sync", lambda e, fw=fw, t=t: e.dma_start(x1sp_d[t], fw[:]),
                   reads=(fwn,), writes=(f"x1sp{t}",), sem="Q", inc=16)
            return fw[:], (fwn,), True
        ln_stats_pipeline(NT, res1_src)
        ln_rows()

        P.claim("R10")
        for t in range(NT):
            x, xn = (R8a, "R8a") if t % 2 == 0 else (R8b, "R8b")
            P.step("sync", lambda e, t=t, x=x: e.dma_start(x[:], x1sp_d[t]),
                   reads=(f"x1sp{t}",), writes=(xn,), sem="Q", inc=16)
            sl = R1[:, t * S:(t + 1) * S]
            P.step("gpsimd", lambda e, sl=sl, x=x: e.tensor_mul(sl, x[:], R9a[:]),
                   reads=(xn, "bcr"), writes=(f"R1_t{t}",))
            P.step("vector", lambda e, sl=sl: e.tensor_sub(sl, sl, R9c[:]),
                   reads=(f"R1_t{t}", "bcmr"), writes=(f"R1_t{t}",))
            if b == 0:
                mid_tasks.pop(12)
        if b == 0:
            mid_tasks.flush()

        P.claim("R2")
        P.claim("R3")
        P.claim("R4")
        W3O = NHT * SQ
        def ffn12_load(hc):
            par = hc % 2
            w1o = par * D
            w2o = 2 * D + par * D
            P.step("scalar", lambda e, hc=hc, w1o=w1o: e.dma_start(
                R3[:, w1o:w1o + D], w1q_d[hc]),
                reads=scratch_reads.get(f"w1q{hc}", (f"w1q{hc}",)),
                writes=(f"R3F_w1{par}",), sem="C", inc=16)
            P.step("scalar", lambda e, hc=hc, w2o=w2o: e.dma_start(
                R3[:, w2o:w2o + D], w2q_d[hc]),
                reads=scratch_reads.get(f"w2q{hc}", (f"w2q{hc}",)),
                writes=(f"R3F_w2{par}",), sem="C", inc=16)

        def ffn3_load(oc):
            yp = oc % 2
            w3o = W3O + yp * FH
            w3reads = sum((scratch_reads.get(f"w3q{oc}_{j}", (f"w3q{oc}_{j}",))
                           for j in range(W3C)), ())
            P.step("scalar", lambda e, oc=oc, w3o=w3o: e.dma_start(
                R2[:, w3o:w3o + FH], w3q_d[oc]),
                reads=w3reads, writes=(f"R2F_w3{yp}",), sem="C", inc=16)

        for sq in range(NSQ):
            if sq == 0:
                ffn12_load(0)
            for hc in range(NHT):
                par = hc % 2
                w1o = par * D
                w2o = 2 * D + par * D
                if hc + 1 < NHT:
                    if sq == 0 or hc >= 1:
                        ffn12_load(hc + 1)
                elif sq + 1 < NSQ:
                    ffn12_load(0)
                if hc == NHT - 2:
                    ffn3_load(0)
                ppair = ((PAs(0), PAs(1), "pa0", "pa1"),
                         (PAs(2), PAs(3), "pa2", "pa3"),
                         (PBs(0), PBs(1), "pb0", "pb1"))[hc % 3]
                ps1, ps2, pn1, pn2 = ppair
                sgs = 3 * SQ + (hc % 2) * SQ
                sgn = f"R4_sg{hc % 2}"
                for wn, wo_, ps_, pn_ in ((f"R3F_w1{par}", w1o, ps1, pn1),
                                          (f"R3F_w2{par}", w2o, ps2, pn2)):
                    fns = []
                    for d in range(NT):
                        fns.append(lambda e, d=d, wo_=wo_, ps_=ps_, sq=sq: e.matmul(
                            ps_, R3[:, wo_ + d * 128: wo_ + (d + 1) * 128],
                            R1[:, d * S + sq * SQ: d * S + sq * SQ + SQ],
                            start=(d == 0), stop=(d == NT - 1)))
                    P.step("tensor", fns, reads=(wn,) + xn_names, writes=(pn_,))
                P.step("scalar",
                       lambda e, ps1=ps1, sgs=sgs: e.activation(
                           R4[:, sgs:sgs + SQ], ps1, AF.Sigmoid,
                           bias=0.0, scale=1.0),
                       reads=(pn1,), writes=(sgn,))
                P.step("vector",
                       lambda e, hc=hc, ps2=ps2, sgs=sgs: e.tensor_mul(
                           R2[:, hc * SQ:(hc + 1) * SQ], R4[:, sgs:sgs + SQ], ps2),
                       reads=(sgn, pn2), writes=(f"R2s{hc}",))
                if b == 0 and sq == 0:
                    ffn_tasks.pop(5)
            if b == 0:
                ffn_tasks.flush()
            for oc in range(NT):
                yp = oc % 2
                w3o = W3O + yp * FH
                if oc + 1 < NT:
                    ffn3_load(oc + 1)
                if oc == 2 and sq + 1 < NSQ:
                    ffn12_load(1)
                fns = []
                for ht in range(NHT):
                    fns.append(lambda e, ht=ht, yp=yp, w3o=w3o: e.matmul(
                        PBs(yp), R2[:, w3o + ht * 128: w3o + (ht + 1) * 128],
                        R2[:, ht * SQ:(ht + 1) * SQ],
                        start=(ht == 0), stop=(ht == NHT - 1)))
                P.step("tensor", fns, reads=(f"R2F_w3{yp}",) + hid_names, writes=(f"pb{yp}",))
                yofs = SQ + yp * SQ
                P.step("scalar",
                       lambda e, yofs=yofs, yp=yp: e.activation(
                           R4[:, yofs:yofs + SQ], PBs(yp), AF.Copy, bias=0.0, scale=1.0),
                       reads=(f"pb{yp}",), writes=(f"R4_y{yp}",))
                P.step("sync",
                       lambda e, yofs=yofs, oc=oc, sq=sq: e.dma_start(
                           ysp_d[oc][:, sq * SQ:(sq + 1) * SQ], R4[:, yofs:yofs + SQ]),
                       reads=(f"R4_y{yp}",), writes=(f"ysp{oc}_{sq}",), sem="Q", inc=16)
                if debug:
                    P.step("gpsimd",
                           lambda e, yofs=yofs, oc=oc, sq=sq: e.dma_start(
                               yspdbg_d[oc][:, sq * SQ:(sq + 1) * SQ], R4[:, yofs:yofs + SQ]),
                           reads=(f"R4_y{yp}",), writes=(f"yspdbg{oc}_{sq}",), sem="G", inc=16)
                P.step("tensor",
                       lambda e, yofs=yofs, oc=oc: e.matmul(
                           PBs(2), ones_bf[:], R4[:, yofs:yofs + SQ],
                           start=(oc == 0), stop=(oc == NT - 1), skip_group_check=True),
                       reads=(f"R4_y{yp}", "ones_bf"), writes=("pb2",))
                P.step("scalar",
                       lambda e, yofs=yofs: e.activation(
                           R6[:, 0:512], R4[:, yofs:yofs + SQ], AF.Square,
                           bias=0.0, scale=1.0),
                       reads=(f"R4_y{yp}",), writes=("exp0",))
                P.step("tensor",
                       lambda e, oc=oc: e.matmul(
                           PBs(3), ones_bf[:], R6[:, 0:512],
                           start=(oc == 0), stop=(oc == NT - 1), skip_group_check=True),
                       reads=("exp0", "ones_bf"), writes=("pb3",))
            c0 = sq * SQ
            P.step("vector",
                   lambda e, c0=c0: e.tensor_scalar(
                       R9a[:, c0:c0 + SQ], PBs(3), 1.0 / D, None, ALU.mult),
                   reads=("pb3",), writes=("bcr",))
            P.step("vector",
                   lambda e, c0=c0: e.tensor_scalar(
                       R8b[:, c0:c0 + SQ], PBs(2), 1.0 / D, None, ALU.mult),
                   reads=("pb2",), writes=("R8b",))
            P.step("vector",
                   lambda e, c0=c0: e.tensor_mul(
                       R9b[:, 0:512], R8b[:, c0:c0 + SQ], R8b[:, c0:c0 + SQ]),
                   reads=("R8b",), writes=("R9b",))
            P.step("vector",
                   lambda e, c0=c0: e.tensor_sub(
                       R9a[:, c0:c0 + SQ], R9a[:, c0:c0 + SQ], R9b[:, 0:512]),
                   reads=("bcr", "R9b"), writes=("bcr",))
            P.step("scalar",
                   lambda e, c0=c0: e.activation(
                       R9b[:, 0:512], R9a[:, c0:c0 + SQ], AF.Ln, bias=epscol[:], scale=1.0),
                   reads=("bcr", "epscol"), writes=("R9b",))
            P.step("scalar",
                   lambda e, c0=c0: e.activation(
                       R9a[:, c0:c0 + SQ], R9b[:, 0:512], AF.Exp, bias=0.0, scale=-0.5),
                   reads=("R9b",), writes=("bcr",))
            P.step("vector",
                   lambda e, c0=c0: e.tensor_mul(
                       R9c[:, c0:c0 + SQ], R8b[:, c0:c0 + SQ], R9a[:, c0:c0 + SQ]),
                   reads=("R8b", "bcr"), writes=("bcmr",))

        if b == 0:
            ffn_tasks.sync_point()

        P.claim("R3")
        P.claim("R10")
        for t in range(NT):
            yofs = (t % 2) * S
            yn = f"R3_y{t % 2}"
            P.step("sync", lambda e, t=t, yofs=yofs: e.dma_start(
                R3[:, yofs:yofs + S], ysp_d[t]),
                reads=tuple(f"ysp{t}_{sq}" for sq in range(NSQ)),
                writes=(yn,), sem="Q", inc=16)
            P.step("sync", lambda e, t=t: e.dma_start(R8a[:], x1sp_d[t]),
                   reads=(f"x1sp{t}",), writes=("R8a",), sem="Q", inc=16)
            fw, fwn = (R10[:, 0:S], "R10") if t % 2 == 0 else (R8b[:], "R8b")
            P.step("gpsimd",
                   lambda e, fw=fw, yofs=yofs: e.tensor_mul(
                       fw[:], R3[:, yofs:yofs + S], R9a[:]),
                   reads=(yn, "bcr"), writes=(fwn,))
            P.step("vector", lambda e, fw=fw: e.tensor_sub(fw[:], fw[:], R9c[:]),
                   reads=(fwn, "bcmr"), writes=(fwn,))
            P.step("vector", lambda e, fw=fw: e.tensor_add(fw[:], fw[:], R8a[:]),
                   reads=(fwn, "R8a"), writes=(fwn,))
            P.step("sync", lambda e, fw=fw, t=t, b=b: e.dma_start(
                out_d[b][t * 128:(t + 1) * 128, :], fw[:]),
                reads=(fwn,), writes=(f"outd{b}_{t}",), sem="Q", inc=16)
        P.claim("R1_")
        P.claim("R2")
        P.claim("R4")

    assert att_tasks.h1 == len(att_tasks.tasks), "gamma tasks left"
    assert mid_tasks.h1 == len(mid_tasks.tasks), "w1/w2 quant tasks left"
    assert ffn_tasks.h1 == len(ffn_tasks.tasks), "w3 quant tasks left"

    for i in range(DMA_POOL):
        P.streams["sync"].append(("w", "Q" + str(i), P.cnt["Q" + str(i)]))
        P.streams["gpsimd"].append(("w", "G" + str(i), P.cnt["G" + str(i)]))
        P.streams["scalar"].append(("w", "C" + str(i), P.cnt["C" + str(i)]))

    with nc.Block() as block:
        def runner(eng_name):
            def go(e):
                for entry in P.streams[eng_name]:
                    if entry[0] == "w":
                        e.wait_ge(sems[entry[1]], entry[2])
                    else:
                        _, f, semname, inc = entry
                        inst = f(e)
                        if semname is not None:
                            inst.then_inc(sems[semname], inc)
            return go
        block.sync(runner("sync"))
        block.gpsimd(runner("gpsimd"))
        block.vector(runner("vector"))
        block.scalar(runner("scalar"))
        block.tensor(runner("tensor"))

    ctxmgr.close()
    return nc


def _perm_idx(H):
    idx = np.zeros(H * 128, np.int64)
    for h in range(H):
        for j in range(64):
            idx[h * 128 + j] = h * 128 + 2 * j
            idx[h * 128 + 64 + j] = h * 128 + 2 * j + 1
    return idx


def _pack(wT, NC, CC):
    Din, Dout = wT.shape
    NTl = Din // 128
    assert Dout == NC * CC
    a = wT.reshape(NTl, 128, NC, CC)
    return np.ascontiguousarray(a.transpose(2, 1, 0, 3).reshape(NC, 128, NTl * CC))


def _trig_table(S):
    j = np.arange(64, dtype=np.float64)
    theta = 10000.0 ** (-j / 64.0)
    pos = np.arange(S, dtype=np.float64)
    fr = pos[None, :] * theta[:, None]
    trig = np.zeros((128, S), np.float64)
    trig[0:64] = np.cos(fr)
    trig[64:128] = np.sin(fr)
    return trig.astype(np.float32)


def pack_core_inputs(cfg, x_core, wq, bq, wk, bk, wv, bv, wo, bo, w1, w2, w3):
    H, NT, NHT, NG, VG = cfg["H"], cfg["NT"], cfg["NHT"], cfg["NG"], cfg["VG"]
    S = cfg["S"]
    perm = _perm_idx(H)
    m = {
        "xt": x_core.transpose(0, 2, 1),
        "wq_p": _pack(np.ascontiguousarray(wq.T[:, perm]), H, 128),
        "wk_p": _pack(np.ascontiguousarray(wk.T[:, perm]), H, 128),
        "wv_p": _pack(np.ascontiguousarray(wv.T), H, 128),
        "wo_p": _pack(np.ascontiguousarray(wo.T), NT, 128),
        "w1_p": _pack(np.ascontiguousarray(w1.T), NHT, 128),
        "w2_p": _pack(np.ascontiguousarray(w2.T), NHT, 128),
        "w3_p": _pack(np.ascontiguousarray(w3.T), NT, 128),
        "bq_p": bq[perm].reshape(H, 128).T,
        "bk_p": bk[perm].reshape(H, 128).T,
        "bv_p": bv.reshape(H, 128).T,
        "bo_p": bo.reshape(NT, 128).T,
        "trig": _trig_table(S),
    }
    return {k: np.ascontiguousarray(v, dtype=np.float32) for k, v in m.items()}


def kernel(L0_attn_x, L0_attn_wq, L0_attn_bq, L0_attn_wk, L0_attn_bk,
           L0_attn_wv, L0_attn_bv, L0_attn_wo, L0_attn_bo,
           L0_ffn_w1, L0_ffn_w2, L0_ffn_w3):
    cfg = make_cfg()
    NCORES = 8
    NB = cfg["NB"]
    x = np.asarray(L0_attn_x, dtype=np.float32)
    wq, bq, wk, bk, wv, bv, wo, bo, w1, w2, w3 = [
        np.asarray(a, dtype=np.float32) for a in
        (L0_attn_wq, L0_attn_bq, L0_attn_wk, L0_attn_bk, L0_attn_wv,
         L0_attn_bv, L0_attn_wo, L0_attn_bo, L0_ffn_w1, L0_ffn_w2, L0_ffn_w3)]

    nc = build_program(cfg)
    in_maps = []
    for c in range(NCORES):
        xc = x[c * NB:(c + 1) * NB]
        in_maps.append(pack_core_inputs(cfg, xc, wq, bq, wk, bk, wv, bv,
                                        wo, bo, w1, w2, w3))
    res = run_bass_kernel_spmd(nc, in_maps, core_ids=list(range(NCORES)))
    outs = [r["out"] for r in res.results]
    full = np.concatenate(outs, axis=0)
    return np.ascontiguousarray(full.transpose(0, 2, 1))
